# version 20
# baseline (speedup 1.0000x reference)
import os
import numpy as np
import ml_dtypes
import concourse.bass as bass
import concourse.mybir as mybir
from concourse.ap import AP
from concourse.bass_utils import run_bass_kernel_spmd
from contextlib import ExitStack

F32 = mybir.dt.float32
BF16 = mybir.dt.bfloat16
AF = mybir.ActivationFunctionType
ALU = mybir.AluOpType
BF = ml_dtypes.bfloat16
AX = mybir.AxisListType.X


class Cfg:
    def __init__(self, S=2048, E=1024, MAXP=1024):
        self.S, self.E, self.MAXP = S, E, MAXP
        self.D = 64
        self.H = E // 64
        self.NPAIR = self.H // 2
        self.JW = 2 * MAXP - 1
        self.PAD = 255
        self.EXTW = self.JW + 2 * self.PAD
        self.NLT = S // 128
        self.LQ = min(512, S)
        self.NLQ = S // self.LQ
        self.NLTQ = self.LQ // 128
        self.NEC = E // 128
        self.SAT = MAXP + 128

    def kind(self, l0, r0):
        if l0 - r0 >= self.SAT:
            return "lo"
        if r0 - l0 >= self.SAT:
            return "hi"
        return "band"


class Buf:
    __slots__ = ("writer", "readers")

    def __init__(self):
        self.writer = None
        self.readers = []


def build_program(cfg, dbg=(), stages=999):
    c = cfg
    S, E, NEC = c.S, c.E, c.NEC
    MAXP, JW, PAD, EXTW = c.MAXP, c.JW, c.PAD, c.EXTW
    LQ, NLQ, NLTQ, NLT = c.LQ, c.NLQ, c.NLTQ, c.NLT
    KWIN = ((LQ + 255 + 511) // 512 + 1) * 512
    nc = bass.Bass("TRN2", target_bir_lowering=False, debug=False)

    def din(name, shape, dt=F32):
        return nc.dram_tensor(name, shape, dt, kind="ExternalInput").ap()

    x_tok = din("x_tok", [S, E])
    x_fm = din("x_fm", [E, S])
    wqT_d = din("wqT", [E, E]); wkT_d = din("wkT", [E, E]); wvT_d = din("wvT", [E, E])
    bq_d = din("bqv", [1, E]); bk_d = din("bkv", [1, E]); bv_d = din("bvv", [1, E])
    eext_d = din("e_ext", [128, EXTW])
    erev_d = din("e_rev", [128, EXTW])
    wr_d = din("wrT", [E, E]); wkf_d = din("wkfT", [E, E]); wvf_d = din("wvfT", [E, E])
    mur_d = din("mur", [128, NEC]); muk_d = din("muk", [128, NEC])
    ident_d = din("ident", [128, 128], BF16)
    identf_d = din("identf", [128, 128], F32)
    ones_d = din("ones_bf", [128, 512], BF16)
    out_t = nc.dram_tensor("out", [S, E], F32, kind="ExternalOutput").ap()
    ctx_dram = nc.dram_tensor("ctx_scr", [S, c.H * 65], F32, kind="Internal").ap()
    ln1_dram = nc.dram_tensor("ln1_scr", [S, E], F32, kind="Internal").ap()
    dbg_used = []

    NS = 8

    class Prog:
        def __init__(self):
            self.eng = {"pe": nc.tensor, "dve": nc.vector, "act": nc.scalar,
                        "sp": nc.sync}
            self.es = ExitStack()
            names = (["pe", "dve", "act", "sp"]
                     + [f"qsp{i}" for i in range(NS)] + [f"qact{i}" for i in range(NS)])
            self.sem = {k: self.es.enter_context(nc.semaphore("sem_" + k))
                        for k in names}
            self.cnt = {k: 0 for k in self.sem}
            self.qtot = {"qsp": 0, "qact": 0}
            self.waited = {}
            self.issuer = {"qsp": "sp", "qact": "act"}

        def _wait(self, eng, dep, val):
            assert val <= self.cnt[dep], f"future wait {eng}<-{dep}@{val}"
            if self.waited.get((eng, dep), -1) >= val:
                return
            self.eng[eng].wait_ge(self.sem[dep], val)
            self.waited[(eng, dep)] = val

        def _pre(self, eng, reads, writes):
            deps = {}

            def add(e, cc):
                if deps.get(e, 0) < cc:
                    deps[e] = cc
            for b in reads:
                if b.writer:
                    add(*b.writer)
            for b in writes:
                if b.writer:
                    add(*b.writer)
                for rc in b.readers:
                    add(*rc)
            for e in sorted(deps):
                if e == eng == "pe":
                    continue
                self._wait(eng, e, deps[e])

        def g(self, eng, reads=(), writes=()):
            pr = self

            class G:
                def __init__(gg):
                    gg.last = None

                def __getattr__(gg, item):
                    fn = getattr(pr.eng[eng], item)

                    def wrap(*a, **k):
                        inst = fn(*a, **k)
                        gg.last = inst
                        return inst
                    return wrap

            class Ctx:
                def __enter__(ctx):
                    pr._pre(eng, reads, writes)
                    ctx.grp = G()
                    return ctx.grp

                def __exit__(ctx, et, ev, tb):
                    if et is not None:
                        return False
                    assert ctx.grp.last is not None
                    ctx.grp.last.then_inc(pr.sem[eng], 1)
                    pr.cnt[eng] += 1
                    mark = (eng, pr.cnt[eng])
                    for b in writes:
                        b.writer = mark
                        b.readers = []
                    for b in reads:
                        b.readers.append(mark)
                    return False
            return Ctx()

        def dma(self, q, out, in_, reads=(), writes=()):
            eng = self.issuer[q]
            s = f"{q}{self.qtot[q] % NS}"
            self.qtot[q] += 1
            self._pre(eng, reads, writes)
            if self.cnt[s] > 0:
                self._wait(eng, s, self.cnt[s])
            self.eng[eng].dma_start(out=out, in_=in_).then_inc(self.sem[s], 16)
            self.cnt[s] += 16
            mark = (s, self.cnt[s])
            for b in writes:
                b.writer = mark
                b.readers = []
            for b in reads:
                b.readers.append(mark)

    pr = Prog()
    b_ctxd = [Buf() for _ in range(NLT)]
    b_ln1d = [Buf() for _ in range(NLT)]
    es2 = ExitStack()
    es3 = ExitStack()

    def sb(n, shape, dt=BF16, es=None):
        return (es or pr.es).enter_context(nc.sbuf_tensor(n, shape, dt))

    def psum(n, shape, dt=F32, es=None):
        return (es or pr.es).enter_context(nc.psum_tensor(n, shape, dt))

    def dump(name, src, shape, dt, reads):
        if name not in dbg:
            return
        o = nc.dram_tensor("dbg_" + name, list(shape), dt, kind="ExternalOutput").ap()
        dbg_used.append(name)
        pr.dma("qsp", out=o, in_=src, reads=list(reads))

    LDW_ = (max(S, E) + 511) // 512 * 512 // 2
    ld = sb("ld", [128, 2 * LDW_], F32)
    b_ld = [Buf(), Buf()]
    wbuf = [sb(f"wbuf{i}", [128, NEC * E]) for i in range(3)]
    b_w = [Buf() for _ in range(3)]
    ident = sb("identity", [128, 128]); b_id = Buf()
    idf32 = sb("idf32", [128, 128], F32); b_idf = Buf()
    ones = sb("ones", [128, 512]); b_1 = Buf()
    stat = sb("stat", [128, 32], F32)
    b_st2 = [Buf(), Buf()]

    xT = sb("xT", [128, NEC * S], es=es2); b_xT = Buf()
    eext = sb("eext", [128, EXTW], es=es2); b_ee = Buf()
    erev = sb("erev", [128, EXTW], es=es2); b_er = Buf()
    biasq = sb("biasq", [1, E], es=es2); b_bq = Buf()
    biask = sb("biask", [1, E], es=es2); b_bk = Buf()
    biasv = sb("biasv", [1, E], es=es2); b_bv = Buf()
    QT = sb("QT", [128, S], es=es2); b_QT = Buf()
    KT = sb("KT", [128, S], es=es2); b_KT = Buf()
    Vsb = sb("Vsb", [128, NLT * 129], es=es2); b_V = Buf()
    scal = sb("scal", [1, 64], F32, es=es2); b_scal = Buf()
    scalb = sb("scalb", [1, 4], es=es2); b_sclb = Buf()
    biasm = sb("biasm", [128, 2], F32, es=es2); b_bm = Buf()
    satr = sb("satr", [128, 4 * S], es=es2); b_sat = Buf()
    qdst = sb("qdst", [128, 2 * EXTW], es=es2)
    b_qd = [[Buf() for _ in range(6)] for _ in range(2)]
    QPE = [sb(f"QPE{h}", [128, NLTQ * S], es=es2) for h in range(2)]
    b_QPE = [[Buf() for _ in range(NLTQ)] for _ in range(2)]
    kdst = sb("kdst", [128, 4 * KWIN], es=es2)
    b_kd_sb = [[[Buf() for _ in range(3)] for _ in range(2)] for _ in range(2)]
    KPE = sb("KPE", [128, 6 * LQ], es=es2)
    b_KPE = [[Buf(), Buf(), Buf()], [Buf(), Buf(), Buf()]]
    PT = sb("PT", [128, 4 * LQ], es=es2); b_PT = [[Buf(), Buf()], [Buf(), Buf()]]
    ctxsb = sb("ctxsb", [65, 2 * LQ], F32, es=es2); b_ctxsb = [Buf(), Buf()]
    trp = sb("trp", [128, 8 * 65], F32, es=es2)
    b_trp = [Buf() for _ in range(8)]

    ps_kd = [psum(f"ps_kd{h}", [128, 512]) for h in range(2)]
    b_kd = [Buf(), Buf()]
    ps_sc = [[psum(f"ps_sc{h}_{sl}", [128, LQ]) for sl in range(2)] for h in range(2)]
    b_sc = [[Buf(), Buf()], [Buf(), Buf()]]
    ps_ctx = [psum(f"ps_ctx{h}", [65, LQ], es=es2) for h in range(2)]
    b_ctx = [Buf(), Buf()]

    def load_cast(dst, b_dst, src, rows, cols):
        slot = 0
        for r0 in range(0, rows, 128):
            rr = min(128, rows - r0)
            for c0 in range(0, cols, LDW_):
                cw = min(LDW_, cols - c0)
                bl = b_ld[slot]
                pr.dma("qsp", out=ld[0:rr, LDW_ * slot:LDW_ * slot + cw],
                       in_=src[r0:r0 + rr, c0:c0 + cw], writes=[bl])
                dcol = (r0 // 128) * cols + c0
                with pr.g("dve", reads=[bl], writes=[b_dst]) as g:
                    g.tensor_copy(dst[0:rr, dcol:dcol + cw],
                                  ld[0:rr, LDW_ * slot:LDW_ * slot + cw])
                slot ^= 1

    pr.dma("qsp", out=ident[:], in_=ident_d, writes=[b_id])
    pr.dma("qsp", out=idf32[:], in_=identf_d, writes=[b_idf])
    pr.dma("qsp", out=ones[:], in_=ones_d, writes=[b_1])
    load_cast(xT, b_xT, x_fm, E, S)
    load_cast(wbuf[0], b_w[0], wqT_d, E, E)
    load_cast(wbuf[1], b_w[1], wkT_d, E, E)
    load_cast(wbuf[2], b_w[2], wvT_d, E, E)
    for (t_, b_, src) in [(biasq, b_bq, bq_d), (biask, b_bk, bk_d), (biasv, b_bv, bv_d)]:
        pr.dma("qsp", out=ld[0:1, 0:E], in_=src, writes=[b_ld[0]])
        with pr.g("dve", reads=[b_ld[0]], writes=[b_]) as g:
            g.tensor_copy(t_[:], ld[0:1, 0:E])
    load_cast(eext, b_ee, eext_d, 128, EXTW)
    load_cast(erev, b_er, erev_d, 128, EXTW)
    for t in range(NLT):
        with pr.g("dve", reads=[b_1], writes=[b_V]) as g:
            g.tensor_copy(Vsb[:, 129 * t + 64:129 * t + 65], ones[:, 0:1])

    def chunk_max(dst_col, row_sb, b_row, hb, width):
        nch = (width + 511) // 512
        for k, cc in enumerate(range(0, width, 512)):
            cw = min(512, width - cc)
            with pr.g("pe", reads=[b_row, b_1], writes=[b_kd[0]]) as g:
                g.matmul(ps_kd[0][0:1, 0:cw], ones[hb:hb + 64, 0:1],
                         row_sb[hb:hb + 64, cc:cc + cw], start=True, stop=True)
            with pr.g("dve", reads=[b_kd[0]], writes=[b_scal]) as g:
                g.tensor_reduce(scal[0:1, 56 + k:57 + k], ps_kd[0][0:1, 0:cw],
                                axis=AX, op=ALU.max)
        with pr.g("dve", reads=[b_scal], writes=[b_scal]) as g:
            g.tensor_reduce(scal[0:1, dst_col:dst_col + 1],
                            scal[0:1, 56:56 + nch], axis=AX, op=ALU.max)

    with pr.g("dve", writes=[x for sub in b_qd for x in sub]) as g:
        g.memset(qdst[:], 0.0)
    with pr.g("dve", writes=[x for sl in b_kd_sb for sub in sl for x in sub]) as g:
        g.memset(kdst[:], 0.0)
    for h in range(2):
        with pr.g("dve", writes=list(b_QPE[h])) as g:
            g.memset(QPE[h][:], 0.0)
    with pr.g("dve", writes=[bb for pair in b_KPE for bb in pair]) as g:
        g.memset(KPE[:], 0.0)
    with pr.g("dve", writes=[bb for pair in b_PT for bb in pair]) as g:
        g.memset(PT[:], 0.0)
    with pr.g("dve", writes=[b_sat]) as g:
        g.memset(satr[:], 0.0)
    with pr.g("dve", writes=[b_scal]) as g:
        g.memset(scal[:], 0.0)

    with pr.g("act", reads=[b_ee], writes=[b_qd[0][0]]) as g:
        g.activation(qdst[0:64, 0:JW], eext[0:64, PAD:PAD + JW], AF.Square)
    chunk_max(32, qdst, b_qd[0][0], 0, JW)

    def projections(p):
        f0 = 128 * p
        for (wi, bias_sb, bb, dstT, bdst) in [(0, biasq, b_bq, QT, b_QT),
                                              (1, biask, b_bk, KT, b_KT)]:
            for lc in range(0, S, 512):
                with pr.g("pe", reads=[b_w[wi], b_xT, bb, b_1], writes=[b_kd[wi]]) as g:
                    for ec in range(NEC):
                        g.matmul(ps_kd[wi][:, 0:512],
                                 wbuf[wi][:, ec * E + f0:ec * E + f0 + 128],
                                 xT[:, ec * S + lc:ec * S + lc + 512],
                                 start=(ec == 0), stop=False)
                    g.matmul(ps_kd[wi][:, 0:512], bias_sb[0:1, f0:f0 + 128],
                             ones[0:1, 0:512], start=False, stop=True)
                with pr.g("dve", reads=[b_kd[wi]], writes=[bdst]) as g:
                    g.tensor_copy(dstT[:, lc:lc + 512], ps_kd[wi][:, 0:512])
        for t in range(NLT):
            sl = t % 2
            with pr.g("pe", reads=[b_w[2], b_xT, b_bv, b_1], writes=[b_kd[sl]]) as g:
                for ec in range(NEC):
                    g.matmul(ps_kd[sl][:, 0:128],
                             xT[:, ec * S + 128 * t:ec * S + 128 * t + 128],
                             wbuf[2][:, ec * E + f0:ec * E + f0 + 128],
                             start=(ec == 0), stop=False)
                g.matmul(ps_kd[sl][:, 0:128], ones[0:1, 0:128],
                         biasv[0:1, f0:f0 + 128], start=False, stop=True)
            with pr.g("dve", reads=[b_kd[sl]], writes=[b_V]) as g:
                g.tensor_copy(Vsb[:, 129 * t + 0:129 * t + 64], ps_kd[sl][:, 0:64])
                g.tensor_copy(Vsb[:, 129 * t + 65:129 * t + 129], ps_kd[sl][:, 64:128])

    def bounds(p):
        for hh in range(2):
            hb = 64 * hh
            o = 16 * hh
            for (T, bT, col) in [(QT, b_QT, 0), (KT, b_KT, 1)]:
                with pr.g("act", reads=[bT], writes=[b_qd[0][0]]) as g:
                    g.activation(qdst[hb:hb + 64, 0:S], T[hb:hb + 64, 0:S], AF.Square)
                chunk_max(o + col, qdst, b_qd[0][0], hb, S)
            with pr.g("act", reads=[b_scal], writes=[b_scal]) as g:
                g.activation(scal[0:1, o + 2:o + 3], scal[0:1, o + 0:o + 1], AF.Ln)
                g.activation(scal[0:1, o + 3:o + 4], scal[0:1, o + 1:o + 2], AF.Ln)
                g.activation(scal[0:1, o + 4:o + 5], scal[0:1, 32:33], AF.Ln)
            with pr.g("dve", reads=[b_scal], writes=[b_scal]) as g:
                g.tensor_tensor(scal[0:1, o + 5:o + 6], scal[0:1, o + 2:o + 3],
                                scal[0:1, o + 3:o + 4], ALU.add)
                g.tensor_tensor(scal[0:1, o + 6:o + 7], scal[0:1, o + 2:o + 3],
                                scal[0:1, o + 4:o + 5], ALU.add)
                g.tensor_tensor(scal[0:1, o + 7:o + 8], scal[0:1, o + 3:o + 4],
                                scal[0:1, o + 4:o + 5], ALU.add)
            with pr.g("act", reads=[b_scal], writes=[b_scal]) as g:
                g.activation(scal[0:1, o + 8:o + 9], scal[0:1, o + 5:o + 6],
                             AF.Exp, scale=0.5)
                g.activation(scal[0:1, o + 9:o + 10], scal[0:1, o + 6:o + 7],
                             AF.Exp, scale=0.5)
                g.activation(scal[0:1, o + 10:o + 11], scal[0:1, o + 7:o + 8],
                             AF.Exp, scale=0.5)
            with pr.g("dve", reads=[b_scal], writes=[b_scal]) as g:
                g.tensor_tensor(scal[0:1, o + 11:o + 12], scal[0:1, o + 8:o + 9],
                                scal[0:1, o + 9:o + 10], ALU.add)
            with pr.g("dve", reads=[b_scal], writes=[b_scal]) as g:
                g.tensor_tensor(scal[0:1, o + 12:o + 13], scal[0:1, o + 11:o + 12],
                                scal[0:1, o + 10:o + 11], ALU.add)
            with pr.g("dve", reads=[b_scal], writes=[b_scal]) as g:
                g.tensor_scalar(scal[0:1, o + 13:o + 14], scal[0:1, o + 12:o + 13],
                                -0.125, None, ALU.mult)
            with pr.g("dve", reads=[b_scal], writes=[b_sclb]) as g:
                g.tensor_copy(scalb[0:1, hh:hh + 1], scal[0:1, o + 13:o + 14])
            with pr.g("pe", reads=[b_sclb, b_1], writes=[b_kd[0]]) as g:
                g.matmul(ps_kd[0][:, 0:1], ones[0:1, 0:128], scalb[0:1, hh:hh + 1],
                         start=True, stop=True)
            with pr.g("dve", reads=[b_kd[0]], writes=[b_bm]) as g:
                g.tensor_copy(biasm[:, hh:hh + 1], ps_kd[0][:, 0:1])

    def sat_rows(p):
        for hh in range(2):
            hb = 64 * hh
            jobs = [(bloc, ecol, T, bT)
                    for (bloc, ecol, T, bT) in [
                        (0, PAD, KT, b_KT), (1, PAD + JW - 1, KT, b_KT),
                        (2, PAD, QT, b_QT), (3, PAD + JW - 1, QT, b_QT)]
                    for cc in [None]]
            k = 0
            for (bloc, ecol, T, bT) in [(0, PAD, KT, b_KT), (1, PAD + JW - 1, KT, b_KT),
                                        (2, PAD, QT, b_QT), (3, PAD + JW - 1, QT, b_QT)]:
                for cc in range(0, S, 512):
                    pst, bt = (ps_kd[k % 2], b_kd[k % 2])
                    with pr.g("pe", reads=[b_ee, bT], writes=[bt]) as g:
                        g.matmul(pst[0:1, 0:512], eext[hb:hb + 64, ecol:ecol + 1],
                                 T[hb:hb + 64, cc:cc + 512], start=True, stop=True)
                    with pr.g("dve", reads=[bt], writes=[b_sat]) as g:
                        g.tensor_copy(
                            satr[hb:hb + 1, bloc * S + cc:bloc * S + cc + 512],
                            pst[0:1, 0:512])
                    k += 1

    def qd_prologue(p, lq):
        l_base = lq * LQ
        for i in range(NLTQ):
            l0 = l_base + 128 * i
            rts = [r0 for r0 in range(0, S, 128) if c.kind(l0, r0) == "band"]
            r_lo, r_hi = rts[0], rts[-1] + 128
            w_lo = r_lo - (l0 + 127) + MAXP - 1 + PAD
            w_hi = (r_hi - 1) - l0 + MAXP - 1 + PAD + 1
            assert 0 <= w_lo and w_hi <= EXTW, (w_lo, w_hi)
            c0 = w_lo
            for ci, cc in enumerate(range(c0, w_hi, 512)):
                cw = min(512, EXTW - cc)
                with pr.g("pe", reads=[b_QT, b_er], writes=[b_kd[0], b_kd[1]]) as g:
                    for hh in range(2):
                        hb = 64 * hh
                        g.matmul(ps_kd[hh][:, 0:cw], QT[hb:hb + 64, l0:l0 + 128],
                                 erev[hb:hb + 64, cc:cc + cw], start=True, stop=True)
                for hh in range(2):
                    ev = "dve" if hh == 0 else "act"
                    dst = qdst[:, EXTW * hh + cc:EXTW * hh + cc + cw]
                    with pr.g(ev, reads=[b_kd[hh]], writes=[b_qd[hh][ci % 6]]) as g:
                        if ev == "dve":
                            g.tensor_copy(dst, ps_kd[hh][:, 0:cw])
                        else:
                            g.copy(dst, ps_kd[hh][:, 0:cw])
            for hh in range(2):
                off = EXTW * hh + (r_lo - l0 + MAXP - 1 + PAD)
                src = AP(qdst.ap().tensor, off, [(2 * EXTW - 1, 128), (1, r_hi - r_lo)])
                pr.dma("qsp", out=QPE[hh][:, S * i + r_lo:S * i + r_hi], in_=src,
                       reads=list(b_qd[hh]), writes=[b_QPE[hh][i]])

    def kd_strip(p, lq, r0, slot):
        l_base = lq * LQ
        lts = [l0 for l0 in range(l_base, l_base + LQ, 128) if c.kind(l0, r0) == "band"]
        if not lts:
            return None
        l_lo, l_hi = lts[0], lts[-1] + 128
        u_lo = l_lo - (r0 + 127) + MAXP - 1 + PAD
        u_hi = (l_hi - 1) - r0 + MAXP - 1 + PAD + 1
        assert 0 <= u_lo and u_hi <= EXTW, (u_lo, u_hi)
        c0 = u_lo
        assert u_hi - c0 <= KWIN
        ksl = slot % 2
        for ci, cc in enumerate(range(c0, u_hi, 512)):
            cw = min(512, EXTW - cc)
            with pr.g("pe", reads=[b_KT, b_ee], writes=[b_kd[0], b_kd[1]]) as g:
                for hh in range(2):
                    hb = 64 * hh
                    g.matmul(ps_kd[hh][:, 0:cw], KT[hb:hb + 64, r0:r0 + 128],
                             eext[hb:hb + 64, cc:cc + cw], start=True, stop=True)
            for hh in range(2):
                ev = "dve" if hh == 0 else "act"
                dst = kdst[:, KWIN * (2 * ksl + hh) + (cc - c0):
                           KWIN * (2 * ksl + hh) + (cc - c0) + cw]
                with pr.g(ev, reads=[b_kd[hh]], writes=[b_kd_sb[ksl][hh][ci]]) as g:
                    if ev == "dve":
                        g.tensor_copy(dst, ps_kd[hh][:, 0:cw])
                    else:
                        g.copy(dst, ps_kd[hh][:, 0:cw])
        for hh in range(2):
            ksl = slot % 2
            off = KWIN * (2 * ksl + hh) + (l_lo - r0 + MAXP - 1 + PAD - c0)
            src = AP(kdst.ap().tensor, off, [(4 * KWIN - 1, 128), (1, l_hi - l_lo)])
            pr.dma("qsp",
                   out=KPE[:, (3 * hh + slot) * LQ:(3 * hh + slot) * LQ + (l_hi - l_lo)],
                   in_=src, reads=list(b_kd_sb[ksl][hh]), writes=[b_KPE[hh][slot]])
        return (l_lo, l_hi)

    def score_pair(p, lq, r0, kwin, kslot, slot):
        l_base = lq * LQ
        opsh = [[], []]
        reads = [b_KT, b_QT, b_id]
        for hh in range(2):
            hb = 64 * hh

            def qk(g, st, sp, hh=hh, hb=hb):
                g.matmul(ps_sc[hh][slot][:, 0:LQ], KT[hb:hb + 64, r0:r0 + 128],
                         QT[hb:hb + 64, l_base:l_base + LQ], start=st, stop=sp)
            opsh[hh].append(qk)
        for (side, kbloc, qbloc) in [("hi", 0, 2), ("lo", 1, 3)]:
            sls = [l0 for l0 in range(l_base, l_base + LQ, 128) if c.kind(l0, r0) == side]
            if not sls:
                continue
            s_lo, s_hi = sls[0] - l_base, sls[-1] + 128 - l_base
            reads += [b_sat, b_1]
            for hh in range(2):
                hb = 64 * hh

                def sat1(g, st, sp, s_lo=s_lo, s_hi=s_hi, kbloc=kbloc, hh=hh, hb=hb):
                    g.matmul(ps_sc[hh][slot][:, s_lo:s_hi],
                             satr[hb:hb + 1, kbloc * S + r0:kbloc * S + r0 + 128],
                             ones[hb:hb + 1, 0:s_hi - s_lo], start=st, stop=sp)

                def sat2(g, st, sp, s_lo=s_lo, s_hi=s_hi, qbloc=qbloc, hh=hh, hb=hb):
                    g.matmul(ps_sc[hh][slot][:, s_lo:s_hi], ones[hb:hb + 1, 0:128],
                             satr[hb:hb + 1, qbloc * S + l_base + s_lo:
                                  qbloc * S + l_base + s_hi], start=st, stop=sp)
                opsh[hh] += [sat1, sat2]
        if kwin is not None:
            l_lo, l_hi = kwin
            for l0 in range(l_lo, l_hi, 128):
                i = (l0 - l_base) // 128
                for hh in range(2):
                    def qop(g, st, sp, l0=l0, i=i, hh=hh):
                        g.matmul(ps_sc[hh][slot][:, l0 - l_base:l0 - l_base + 128],
                                 QPE[hh][:, S * i + r0:S * i + r0 + 128], ident[:],
                                 start=st, stop=sp)
                    opsh[hh].append(qop)
                reads += [b_QPE[0][i], b_QPE[1][i]]
        dedup, seen = [], set()
        for b in reads:
            if id(b) not in seen:
                seen.add(id(b))
                dedup.append(b)
        nops = len(opsh[0])
        with pr.g("pe", reads=dedup, writes=[b_sc[0][slot], b_sc[1][slot]]) as g:
            for k in range(nops):
                for hh in range(2):
                    opsh[hh][k](g, k == 0, k == nops - 1 and kwin is None)
        if kwin is not None:
            with pr.g("pe", reads=[b_KPE[0][kslot], b_KPE[1][kslot], b_id],
                      writes=[b_sc[0][slot], b_sc[1][slot]]) as g:
                for hh in range(2):
                    g.matmul(ps_sc[hh][slot][:, l_lo - l_base:l_hi - l_base], ident[:],
                             KPE[:, (3 * hh + kslot) * LQ:(3 * hh + kslot) * LQ + (l_hi - l_lo)],
                             start=False, stop=True)

    def exp_tile(p, lq, r0, hh, slot):
        with pr.g("act", reads=[b_sc[hh][slot], b_bm], writes=[b_PT[hh][slot]]) as g:
            g.activation(PT[:, (2 * hh + slot) * LQ:(2 * hh + slot) * LQ + LQ],
                         ps_sc[hh][slot][:, 0:LQ], AF.Exp,
                         bias=biasm[:, hh:hh + 1], scale=0.125)

    def pv_pair(p, lq, r0, slot, first, last):
        t = r0 // 128
        with pr.g("pe", reads=[b_V, b_PT[0][slot], b_PT[1][slot]],
                  writes=[b_ctx[0], b_ctx[1]]) as g:
            for hh in range(2):
                vcol = 129 * t + (0 if hh == 0 else 64)
                g.matmul(ps_ctx[hh][:, 0:LQ], Vsb[:, vcol:vcol + 65],
                         PT[:, (2 * hh + slot) * LQ:(2 * hh + slot) * LQ + LQ],
                         start=first, stop=last)

    def ctx_out(p, lq):
        l_base = lq * LQ
        for hh in range(2):
            h = 2 * p + hh
            with pr.g("dve", reads=[b_ctx[hh]], writes=[b_ctxsb[hh]]) as g:
                g.tensor_copy(ctxsb[:, hh * LQ:hh * LQ + LQ], ps_ctx[hh][:, 0:LQ])
            for i in range(NLTQ):
                l0 = l_base + 128 * i
                tsl = 4 * hh + i % 4
                with pr.g("pe", reads=[b_ctxsb[hh], b_idf], writes=[b_kd[hh]]) as g:
                    g.transpose(ps_kd[hh][:, 0:65],
                                ctxsb[0:65, hh * LQ + 128 * i:hh * LQ + 128 * i + 128],
                                idf32[0:65, 0:65])
                with pr.g("dve", reads=[b_kd[hh]], writes=[b_trp[tsl]]) as g:
                    g.tensor_copy(trp[:, 65 * tsl:65 * tsl + 65], ps_kd[hh][:, 0:65])
                pr.dma("qsp", out=ctx_dram[l0:l0 + 128, 65 * h:65 * h + 65],
                       in_=trp[:, 65 * tsl:65 * tsl + 65],
                       reads=[b_trp[tsl], b_ctxd[l0 // 128]])

    for p in range(c.NPAIR if stages >= 90 else (1 if stages >= 2 else 0)):
        projections(p)
        if stages >= 3:
            bounds(p)
        if stages >= 4:
            sat_rows(p)
        if p == 0:
            dump("QT0", QT[:], (128, S), BF16, [b_QT])
            dump("KT0", KT[:], (128, S), BF16, [b_KT])
            dump("Vsb", Vsb[:], (128, NLT * 129), BF16, [b_V])
            dump("satr", satr[:], (128, 4 * S), BF16, [b_sat])
            dump("biasm", biasm[:], (128, 2), F32, [b_bm])
        for lq in range(NLQ if stages >= 80 else (1 if stages >= 5 else 0)):
            pass_prologue = True
            qd_prologue(p, lq)
            if p == 0 and lq == 0:
                dump("QPE0", QPE[0][:], (128, NLTQ * S), BF16, b_QPE[0])
            rmax = S if stages >= 80 else (128 if stages >= 6 else 0)
            l_mid = lq * LQ + LQ // 2
            r0s = sorted(range(0, rmax, 128), key=lambda r: -abs(r + 64 - l_mid))
            kwins = {}
            if rmax:
                kwins[r0s[0]] = kd_strip(p, lq, r0s[0], 0)
            if len(r0s) > 1:
                kwins[r0s[1]] = kd_strip(p, lq, r0s[1], 1)
            for ri, r0 in enumerate(r0s):
                slot = ri % 2
                if p == 0 and lq == 0 and r0 == 0:
                    dump("KPE0", KPE[:], (128, 6 * LQ), BF16,
                         [bb for pair in b_KPE for bb in pair])
                if ri + 2 < len(r0s):
                    kwins[r0s[ri + 2]] = kd_strip(p, lq, r0s[ri + 2], (ri + 2) % 3)
                if stages >= 7:
                    score_pair(p, lq, r0, kwins[r0], ri % 3, slot)
                    if stages >= 71:
                        for hh in range(2):
                            exp_tile(p, lq, r0, hh, slot)
                if stages >= 72:
                    pv_pair(p, lq, r0, slot, ri == 0, ri == len(r0s) - 1)
                if p == 0 and lq == 0 and r0 == 0:
                    dump("PT0", PT[:], (128, 4 * LQ), BF16,
                         [bb for pair in b_PT for bb in pair])
            if stages >= 80:
                ctx_out(p, lq)

    es2.close()

    resid = sb("resid", [128, 2 * E], F32, es=es3)
    b_res2 = [Buf(), Buf()]
    sqf = sb("sqf", [128, 2 * E], F32, es=es3)
    b_sqf2 = [Buf(), Buf()]
    ctxt = sb("ctxt", [128, 2 * c.H * 65], F32, es=es3)
    b_ct2 = [Buf(), Buf()]
    xt_sb = sb("xt_sb", [128, 4 * E], F32, es=es3)
    b_xts2 = [Buf() for _ in range(4)]
    ln1bf = sb("ln1bf", [128, 2 * E], es=es3)
    b_l1b2 = [Buf(), Buf()]
    ln1T = sb("ln1T", [128, NEC * S], es=es3)
    b_l1T = [Buf() for _ in range(NLT)]
    ps_tb = psum("ps_tb", [128, 2048], BF16, es=es3)
    b_tb = [Buf(), Buf()]

    def ln_stats(b_src, src_sb, sl=0):
        b_st = b_st2[sl]
        so = 16 * sl
        nh = (E + 511) // 512
        with pr.g("dve", reads=[b_src], writes=[b_st]) as g:
            for hhh in range(nh):
                w = min(512, E - 512 * hhh)
                g.bn_stats(stat[:, so + 6 * hhh:so + 6 * hhh + 6],
                           src_sb[:, 512 * hhh:512 * hhh + w])
        with pr.g("dve", reads=[b_st], writes=[b_st]) as g:
            g.bn_aggr(stat[:, so + 12:so + 14], stat[:, so:so + 6 * nh])
        with pr.g("dve", reads=[b_st], writes=[b_st]) as g:
            g.tensor_scalar(stat[:, so + 14:so + 15], stat[:, so + 13:so + 14], 1.0,
                            1e-5, ALU.mult, ALU.add)
        with pr.g("act", reads=[b_st], writes=[b_st]) as g:
            g.activation(stat[:, so + 15:so + 16], stat[:, so + 14:so + 15], AF.Ln)
        with pr.g("act", reads=[b_st], writes=[b_st]) as g:
            g.activation(stat[:, so + 10:so + 11], stat[:, so + 15:so + 16],
                         AF.Exp, scale=-0.5)
        with pr.g("dve", reads=[b_src, b_st], writes=[b_src]) as g:
            g.tensor_scalar(src_sb, src_sb, stat[:, so + 12:so + 13],
                            stat[:, so + 10:so + 11], ALU.subtract, ALU.mult)

    HC = c.H * 65

    def ln1_tile(t):
        l0 = 128 * t
        sl = t % 2
        b_xts, b_ct = b_xts2[sl], b_ct2[sl]
        b_res = b_res2[sl]
        rs_v = resid[:, sl * E:sl * E + E]
        pr.dma("qsp", out=xt_sb[:, sl * E:sl * E + E], in_=x_tok[l0:l0 + 128, 0:E],
               writes=[b_xts])
        pr.dma("qsp", out=ctxt[:, sl * HC:sl * HC + HC],
               in_=ctx_dram[l0:l0 + 128, 0:HC], writes=[b_ct, b_ctxd[t]])
        with pr.g("dve", reads=[b_ct], writes=[b_ct]) as g:
            for h in range(c.H):
                zc = sl * HC + 65 * h + (64 if h % 2 == 0 else 0)
                g.reciprocal(ctxt[:, zc:zc + 1], ctxt[:, zc:zc + 1])
        with pr.g("dve", reads=[b_ct], writes=[b_res]) as g:
            for h in range(c.H):
                zc = sl * HC + 65 * h + (64 if h % 2 == 0 else 0)
                cc = sl * HC + 65 * h + (0 if h % 2 == 0 else 1)
                g.tensor_scalar(rs_v[:, 64 * h:64 * h + 64], ctxt[:, cc:cc + 64],
                                ctxt[:, zc:zc + 1], None, ALU.mult)
        with pr.g("dve", reads=[b_xts, b_res], writes=[b_res]) as g:
            g.tensor_tensor(rs_v, rs_v, xt_sb[:, sl * E:sl * E + E], ALU.add)
        ln_stats(b_res, rs_v, sl)
        pr.dma("qsp", out=ln1_dram[l0:l0 + 128, 0:E], in_=rs_v, reads=[b_res],
               writes=[b_ln1d[t]])
        with pr.g("dve", reads=[b_res], writes=[b_l1b2[sl]]) as g:
            g.tensor_copy(ln1bf[:, sl * E:sl * E + E], rs_v)
        for ec in range(NEC):
            esl = ec % 2
            with pr.g("pe", reads=[b_l1b2[sl], b_id], writes=[b_tb[esl]]) as g:
                g.transpose(ps_tb[:, 1024 * esl:1024 * esl + 128],
                            ln1bf[:, sl * E + 128 * ec:sl * E + 128 * ec + 128], ident[:])
            with pr.g("dve", reads=[b_tb[esl]], writes=[b_l1T[t]]) as g:
                g.tensor_copy(ln1T[:, ec * S + l0:ec * S + l0 + 128],
                              ps_tb[:, 1024 * esl:1024 * esl + 128])
        if t == 0:
            dump("ln1_0", resid[:, 0:E], (128, E), F32, [b_res2[0]])

    load_cast(wbuf[0], b_w[0], wr_d, E, E)
    load_cast(wbuf[1], b_w[1], wkf_d, E, E)
    load_cast(wbuf[2], b_w[2], wvf_d, E, E)
    mur = sb("mur_sb", [128, NEC], F32, es=es3); b_mur = Buf()
    muk = sb("muk_sb", [128, NEC], F32, es=es3); b_muk = Buf()
    murc = sb("murc", [128, 2 * NEC], F32, es=es3); b_murc = Buf()
    for (t_, b_, src) in [(mur, b_mur, mur_d), (muk, b_muk, muk_d)]:
        pr.dma("qsp", out=ld[0:128, 0:NEC], in_=src, writes=[b_ld[0]])
        with pr.g("dve", reads=[b_ld[0]], writes=[b_]) as g:
            g.tensor_copy(t_[:], ld[0:128, 0:NEC])
    with pr.g("dve", reads=[b_mur, b_muk], writes=[b_murc]) as g:
        g.tensor_scalar(murc[:, 0:NEC], mur[:], -1.0, 1.0, ALU.mult, ALU.add)
        g.tensor_scalar(murc[:, NEC:2 * NEC], muk[:], -1.0, 1.0, ALU.mult, ALU.add)

    LB = LQ
    xrT = sb("xrT", [128, NEC * LB], es=es3); b_xr = Buf()
    xkT = sb("xkT", [128, NEC * LB], es=es3); b_xk = Buf()
    rsb = sb("rsb", [128, NEC * LB], F32, es=es3); b_r = Buf()
    kact = sb("kact", [128, NEC * LB], es=es3); b_ka = Buf()
    ffnT = sb("ffnT", [128, NEC * LB], F32, es=es3); b_ff = Buf()
    otile = sb("otile", [128, (512 // 128) * E], F32, es=es3); b_ot = Buf()

    def ffn_block(bl):
        l0 = bl * LB
        t_lo = max(0, bl * LB // 128 - 1)
        t_hi = min(NLT, (bl * LB + LB) // 128)
        l1deps = [b_l1T[tt] for tt in range(t_lo, t_hi)]
        with pr.g("dve", reads=l1deps + [b_murc], writes=[b_xr, b_xk]) as g:
            for ec in range(NEC):
                if l0 == 0:
                    g.memset(xrT[:, ec * LB:ec * LB + 1], 0.0)
                    g.memset(xkT[:, ec * LB:ec * LB + 1], 0.0)
                    d0, sh = 1, 0
                else:
                    d0, sh = 0, -1
                g.tensor_scalar(xrT[:, ec * LB + d0:ec * LB + LB],
                                ln1T[:, ec * S + l0 + sh:ec * S + l0 + LB - 1],
                                murc[:, ec:ec + 1], None, ALU.mult)
                g.tensor_scalar(xkT[:, ec * LB + d0:ec * LB + LB],
                                ln1T[:, ec * S + l0 + sh:ec * S + l0 + LB - 1],
                                murc[:, NEC + ec:NEC + ec + 1], None, ALU.mult)
        with pr.g("dve", reads=l1deps + [b_mur, b_muk, b_xr, b_xk],
                  writes=[b_xr, b_xk]) as g:
            for ec in range(NEC):
                g.scalar_tensor_tensor(xrT[:, ec * LB:ec * LB + LB],
                                       ln1T[:, ec * S + l0:ec * S + l0 + LB],
                                       mur[:, ec:ec + 1], xrT[:, ec * LB:ec * LB + LB],
                                       ALU.mult, ALU.add)
                g.scalar_tensor_tensor(xkT[:, ec * LB:ec * LB + LB],
                                       ln1T[:, ec * S + l0:ec * S + l0 + LB],
                                       muk[:, ec:ec + 1], xkT[:, ec * LB:ec * LB + LB],
                                       ALU.mult, ALU.add)
        for dc in range(NEC):
            pr1, br1 = (ps_kd[0], b_kd[0]) if dc % 2 == 0 else (ps_sc[0][0], b_sc[0][0])
            pr2, br2 = (ps_kd[1], b_kd[1]) if dc % 2 == 0 else (ps_sc[1][0], b_sc[1][0])
            with pr.g("pe", reads=[b_w[0], b_xr], writes=[br1]) as g:
                for ec in range(NEC):
                    g.matmul(pr1[:, 0:LB],
                             wbuf[0][:, ec * E + 128 * dc:ec * E + 128 * dc + 128],
                             xrT[:, ec * LB:ec * LB + LB],
                             start=(ec == 0), stop=(ec == NEC - 1))
            with pr.g("act", reads=[br1], writes=[b_r]) as g:
                g.activation(rsb[:, dc * LB:dc * LB + LB], pr1[:, 0:LB], AF.Sigmoid)
            with pr.g("pe", reads=[b_w[1], b_xk], writes=[br2]) as g:
                for ec in range(NEC):
                    g.matmul(pr2[:, 0:LB],
                             wbuf[1][:, ec * E + 128 * dc:ec * E + 128 * dc + 128],
                             xkT[:, ec * LB:ec * LB + LB],
                             start=(ec == 0), stop=(ec == NEC - 1))
            with pr.g("act", reads=[br2], writes=[b_ka]) as g:
                g.activation(kact[:, dc * LB:dc * LB + LB], pr2[:, 0:LB], AF.Relu)
            with pr.g("act", reads=[b_ka], writes=[b_ka]) as g:
                g.activation(kact[:, dc * LB:dc * LB + LB],
                             kact[:, dc * LB:dc * LB + LB], AF.Square)
        NSUB = LB // 128
        for i in range(NSUB):
            lt = l0 + 128 * i
            pr.dma("qsp", out=xt_sb[:, i * E:i * E + E],
                   in_=ln1_dram[lt:lt + 128, 0:E], reads=[b_ln1d[lt // 128]],
                   writes=[b_xts2[i]])
        b_ffc = [Buf() for _ in range(NEC)]
        for dc in range(NEC):
            psv, bsv = (ps_sc[0][1], b_sc[0][1]) if dc % 2 == 0 else (ps_sc[1][1], b_sc[1][1])
            with pr.g("pe", reads=[b_w[2], b_ka], writes=[bsv]) as g:
                for ec in range(NEC):
                    g.matmul(psv[:, 0:LB],
                             wbuf[2][:, ec * E + 128 * dc:ec * E + 128 * dc + 128],
                             kact[:, ec * LB:ec * LB + LB],
                             start=(ec == 0), stop=(ec == NEC - 1))
            with pr.g("dve", reads=[bsv, b_r], writes=[b_ffc[dc]]) as g:
                g.tensor_tensor(ffnT[:, dc * LB:dc * LB + LB], psv[:, 0:LB],
                                rsb[:, dc * LB:dc * LB + LB], ALU.mult)
            for i in range(NSUB):
                sl = (dc + i) % 2
                with pr.g("pe", reads=[b_ffc[dc], b_idf], writes=[b_kd[sl]]) as g:
                    g.transpose(ps_kd[sl][:, 0:128],
                                ffnT[0:128, dc * LB + 128 * i:dc * LB + 128 * i + 128],
                                idf32[:])
                with pr.g("dve", reads=[b_kd[sl], b_xts2[i]], writes=[b_ot]) as g:
                    g.tensor_tensor(otile[:, i * E + 128 * dc:i * E + 128 * dc + 128],
                                    ps_kd[sl][:, 0:128],
                                    xt_sb[:, i * E + 128 * dc:i * E + 128 * dc + 128],
                                    ALU.add)
        for i in range(NSUB):
            lt = l0 + 128 * i
            ln_stats(b_ot, otile[:, i * E:i * E + E], i % 2)
            pr.dma("qsp", out=out_t[lt:lt + 128, 0:E], in_=otile[:, i * E:i * E + E],
                   reads=[b_ot])
        if bl == 0:
            dump("xrT0", xrT[:], (128, NEC * LB), BF16, [b_xr])
            dump("ffnT0", ffnT[:], (128, NEC * LB), F32, b_ffc)

    TPB = LB // 128
    if stages >= 100:
        for t in range(NLT):
            ln1_tile(t)
            if stages >= 110 and (t + 1) % TPB == 0:
                ffn_block((t + 1) // TPB - 1)

    for i in range(4):
        if pr.cnt[f"qsp{i}"]:
            nc.sync.wait_ge(pr.sem[f"qsp{i}"], pr.cnt[f"qsp{i}"])
        if pr.cnt[f"qact{i}"]:
            nc.sync.wait_ge(pr.sem[f"qact{i}"], pr.cnt[f"qact{i}"])
    es3.close()
    pr.es.close()
    return nc, dbg_used


def host_inputs(cfg, x_b, wq, bq, wk, bk, wv, bv, dist, W_r, W_k, W_v, mu_r, mu_k):
    c = cfg
    ar = lambda a: np.ascontiguousarray(a, np.float32)
    idx = np.arange(c.EXTW) - c.PAD
    e_ext = dist[np.clip(idx, 0, c.JW - 1)]
    e_rev = dist[np.clip(c.JW - 1 - idx, 0, c.JW - 1)]
    em = {
        "x_tok": ar(x_b), "x_fm": ar(x_b.T),
        "wqT": ar(wq.T), "wkT": ar(wk.T), "wvT": ar(wv.T),
        "bqv": ar(bq.reshape(1, -1)), "bkv": ar(bk.reshape(1, -1)),
        "bvv": ar(bv.reshape(1, -1)),
        "e_ext": ar(np.concatenate([e_ext.T, e_ext.T], axis=0)),
        "e_rev": ar(np.concatenate([e_rev.T, e_rev.T], axis=0)),
        "wrT": ar(W_r.T), "wkfT": ar(W_k.T), "wvfT": ar(W_v.T),
        "mur": ar(mu_r.reshape(c.NEC, 128).T),
        "muk": ar(mu_k.reshape(c.NEC, 128).T),
        "ident": np.eye(128, dtype=BF),
        "identf": np.eye(128, dtype=np.float32),
        "ones_bf": np.ones((128, 512), dtype=BF),
    }
    return em


def kernel(L0_attn_hidden_states, L0_attn_wq, L0_attn_bq, L0_attn_wk, L0_attn_bk,
           L0_attn_wv, L0_attn_bv, L0_attn_dist_emb, L0_ffn_W_r, L0_ffn_W_k,
           L0_ffn_W_v, L0_ffn_mu_r, L0_ffn_mu_k):
    cfg = Cfg(S=2048, E=1024, MAXP=1024)
    f = lambda a: np.asarray(a, np.float32)
    x = np.ascontiguousarray(f(L0_attn_hidden_states))
    nc, _ = build_program(cfg)
    in_maps = [host_inputs(cfg, x[b], f(L0_attn_wq), f(L0_attn_bq), f(L0_attn_wk),
                           f(L0_attn_bk), f(L0_attn_wv), f(L0_attn_bv),
                           f(L0_attn_dist_emb), f(L0_ffn_W_r), f(L0_ffn_W_k),
                           f(L0_ffn_W_v), f(L0_ffn_mu_r), f(L0_ffn_mu_k))
               for b in range(8)]
    res = run_bass_kernel_spmd(nc, in_maps, core_ids=list(range(8)))
    return np.stack([np.asarray(res.results[b]["out"], np.float32) for b in range(8)])


# version 24 (final)
# speedup vs baseline: 1.3009x; 1.3009x over previous
import os
import numpy as np
import ml_dtypes
import concourse.bass as bass
import concourse.mybir as mybir
from concourse.ap import AP
from concourse.bass_utils import run_bass_kernel_spmd
from contextlib import ExitStack

F32 = mybir.dt.float32
BF16 = mybir.dt.bfloat16
AF = mybir.ActivationFunctionType
ALU = mybir.AluOpType
BF = ml_dtypes.bfloat16
AX = mybir.AxisListType.X


class Cfg:
    def __init__(self, S=2048, E=1024, MAXP=1024):
        self.S, self.E, self.MAXP = S, E, MAXP
        self.D = 64
        self.H = E // 64
        self.NPAIR = self.H // 2
        self.JW = 2 * MAXP - 1
        self.PAD = 255
        self.EXTW = self.JW + 2 * self.PAD
        self.NLT = S // 128
        self.LQ = min(512, S)
        self.NLQ = S // self.LQ
        self.NLTQ = self.LQ // 128
        self.NEC = E // 128
        self.SAT = MAXP + 128

    def kind(self, l0, r0):
        if l0 - r0 >= self.SAT:
            return "lo"
        if r0 - l0 >= self.SAT:
            return "hi"
        return "band"


class Buf:
    __slots__ = ("writer", "readers")

    def __init__(self):
        self.writer = None
        self.readers = []


def build_program(cfg, dbg=(), stages=999):
    c = cfg
    S, E, NEC = c.S, c.E, c.NEC
    MAXP, JW, PAD, EXTW = c.MAXP, c.JW, c.PAD, c.EXTW
    LQ, NLQ, NLTQ, NLT = c.LQ, c.NLQ, c.NLTQ, c.NLT
    KWIN = ((LQ + 255 + 511) // 512 + 1) * 512
    nc = bass.Bass("TRN2", target_bir_lowering=False, debug=False)

    def din(name, shape, dt=F32):
        return nc.dram_tensor(name, shape, dt, kind="ExternalInput").ap()

    x_tok = din("x_tok", [S, E])
    x_fm = din("x_fm", [E, S])
    wqT_d = din("wqT", [E, E]); wkT_d = din("wkT", [E, E]); wvT_d = din("wvT", [E, E])
    bq_d = din("bqv", [1, E]); bk_d = din("bkv", [1, E]); bv_d = din("bvv", [1, E])
    eext_d = din("e_ext", [128, EXTW])
    erev_d = din("e_rev", [128, EXTW])
    wr_d = din("wrT", [E, E]); wkf_d = din("wkfT", [E, E]); wvf_d = din("wvfT", [E, E])
    mur_d = din("mur", [128, NEC]); muk_d = din("muk", [128, NEC])
    ident_d = din("ident", [128, 128], BF16)
    identf_d = din("identf", [128, 128], F32)
    ones_d = din("ones_bf", [128, 512], BF16)
    out_t = nc.dram_tensor("out", [S, E], F32, kind="ExternalOutput").ap()
    ctx_dram = nc.dram_tensor("ctx_scr", [S, c.H * 65], F32, kind="Internal").ap()
    ln1_dram = nc.dram_tensor("ln1_scr", [S, E], F32, kind="Internal").ap()
    dbg_used = []

    NS = 8

    class Prog:
        def __init__(self):
            self.eng = {"pe": nc.tensor, "dve": nc.vector, "act": nc.scalar,
                        "sp": nc.sync}
            self.es = ExitStack()
            names = (["pe", "dve", "act", "sp"]
                     + [f"qsp{i}" for i in range(NS)] + [f"qact{i}" for i in range(NS)])
            self.sem = {k: self.es.enter_context(nc.semaphore("sem_" + k))
                        for k in names}
            self.cnt = {k: 0 for k in self.sem}
            self.qtot = {"qsp": 0, "qact": 0}
            self.waited = {}
            self.issuer = {"qsp": "sp", "qact": "act"}

        def _wait(self, eng, dep, val):
            assert val <= self.cnt[dep], f"future wait {eng}<-{dep}@{val}"
            if self.waited.get((eng, dep), -1) >= val:
                return
            self.eng[eng].wait_ge(self.sem[dep], val)
            self.waited[(eng, dep)] = val

        def _pre(self, eng, reads, writes):
            deps = {}

            def add(e, cc):
                if deps.get(e, 0) < cc:
                    deps[e] = cc
            for b in reads:
                if b.writer:
                    add(*b.writer)
            for b in writes:
                if b.writer:
                    add(*b.writer)
                for rc in b.readers:
                    add(*rc)
            for e in sorted(deps):
                if e == eng == "pe":
                    continue
                self._wait(eng, e, deps[e])

        def g(self, eng, reads=(), writes=()):
            pr = self

            class G:
                def __init__(gg):
                    gg.last = None

                def __getattr__(gg, item):
                    fn = getattr(pr.eng[eng], item)

                    def wrap(*a, **k):
                        inst = fn(*a, **k)
                        gg.last = inst
                        return inst
                    return wrap

            class Ctx:
                def __enter__(ctx):
                    pr._pre(eng, reads, writes)
                    ctx.grp = G()
                    return ctx.grp

                def __exit__(ctx, et, ev, tb):
                    if et is not None:
                        return False
                    assert ctx.grp.last is not None
                    ctx.grp.last.then_inc(pr.sem[eng], 1)
                    pr.cnt[eng] += 1
                    mark = (eng, pr.cnt[eng])
                    for b in writes:
                        b.writer = mark
                        b.readers = []
                    for b in reads:
                        b.readers.append(mark)
                    return False
            return Ctx()

        def dma(self, q, out, in_, reads=(), writes=()):
            eng = self.issuer[q]
            s = f"{q}{self.qtot[q] % NS}"
            self.qtot[q] += 1
            self._pre(eng, reads, writes)
            if self.cnt[s] > 0:
                self._wait(eng, s, self.cnt[s])
            self.eng[eng].dma_start(out=out, in_=in_).then_inc(self.sem[s], 16)
            self.cnt[s] += 16
            mark = (s, self.cnt[s])
            for b in writes:
                b.writer = mark
                b.readers = []
            for b in reads:
                b.readers.append(mark)

    pr = Prog()
    b_ctxd = [Buf() for _ in range(NLT)]
    b_ln1d = [Buf() for _ in range(NLT)]
    es2 = ExitStack()
    es3 = ExitStack()

    def sb(n, shape, dt=BF16, es=None):
        return (es or pr.es).enter_context(nc.sbuf_tensor(n, shape, dt))

    def psum(n, shape, dt=F32, es=None):
        return (es or pr.es).enter_context(nc.psum_tensor(n, shape, dt))

    def dump(name, src, shape, dt, reads):
        if name not in dbg:
            return
        o = nc.dram_tensor("dbg_" + name, list(shape), dt, kind="ExternalOutput").ap()
        dbg_used.append(name)
        pr.dma("qsp", out=o, in_=src, reads=list(reads))

    LDW_ = (max(S, E) + 511) // 512 * 512 // 2
    ld = sb("ld", [128, 2 * LDW_], F32)
    b_ld = [Buf(), Buf()]
    wbuf = [sb(f"wbuf{i}", [128, NEC * E]) for i in range(3)]
    b_w = [Buf() for _ in range(3)]
    ident = sb("identity", [128, 128]); b_id = Buf()
    idf32 = sb("idf32", [128, 128], F32); b_idf = Buf()
    ones = sb("ones", [128, 512]); b_1 = Buf()
    stat = sb("stat", [128, 32], F32)
    b_st2 = [Buf(), Buf()]

    xT = sb("xT", [128, NEC * S], es=es2); b_xT = Buf()
    eext = sb("eext", [128, EXTW], es=es2); b_ee = Buf()
    erev = sb("erev", [128, EXTW], es=es2); b_er = Buf()
    biasq = sb("biasq", [1, E], es=es2); b_bq = Buf()
    biask = sb("biask", [1, E], es=es2); b_bk = Buf()
    biasv = sb("biasv", [1, E], es=es2); b_bv = Buf()
    QT = sb("QT", [128, S], es=es2); b_QT = Buf()
    KT = sb("KT", [128, S], es=es2); b_KT = Buf()
    Vsb = sb("Vsb", [128, NLT * 129], es=es2); b_V = Buf()
    scal = sb("scal", [1, 64], F32, es=es2); b_scal = Buf()
    scalb = sb("scalb", [1, 4], es=es2); b_sclb = Buf()
    biasm = sb("biasm", [128, 2], F32, es=es2); b_bm = Buf()
    satr = sb("satr", [128, 4 * S], es=es2); b_sat = Buf()
    qdst = sb("qdst", [128, 2 * EXTW], es=es2)
    b_qd = [[Buf() for _ in range(6)] for _ in range(2)]
    QPE = [sb(f"QPE{h}", [128, NLTQ * S], es=es2) for h in range(2)]
    b_QPE = [[Buf() for _ in range(NLTQ)] for _ in range(2)]
    kdst = sb("kdst", [128, 4 * KWIN], es=es2)
    b_kd_sb = [[[Buf() for _ in range(3)] for _ in range(2)] for _ in range(2)]
    KPE = sb("KPE", [128, 6 * LQ], es=es2)
    b_KPE = [[Buf(), Buf(), Buf()], [Buf(), Buf(), Buf()]]
    PT = sb("PT", [128, 4 * LQ], es=es2); b_PT = [[Buf(), Buf()], [Buf(), Buf()]]
    ctxsb = sb("ctxsb", [65, 2 * LQ], F32, es=es2); b_ctxsb = [Buf(), Buf()]
    trp = sb("trp", [128, 8 * 65], F32, es=es2)
    b_trp = [Buf() for _ in range(8)]

    ps_kd = [psum(f"ps_kd{h}", [128, 512]) for h in range(2)]
    b_kd = [Buf(), Buf()]
    ps_qd = [psum(f"ps_qd{h}", [128, 512]) for h in range(2)]
    b_qdp = [Buf(), Buf()]
    _pssc = [psum(f"ps_sc{h}", [128, LQ]) for h in range(2)]
    _bsc = [Buf(), Buf()]
    ps_sc = [[_pssc[h], _pssc[h]] for h in range(2)]
    b_sc = [[_bsc[h], _bsc[h]] for h in range(2)]
    ps_ctx = [psum(f"ps_ctx{h}", [65, LQ], es=es2) for h in range(2)]
    b_ctx = [Buf(), Buf()]

    def load_cast(dst, b_dst, src, rows, cols):
        slot = 0
        for r0 in range(0, rows, 128):
            rr = min(128, rows - r0)
            for c0 in range(0, cols, LDW_):
                cw = min(LDW_, cols - c0)
                bl = b_ld[slot]
                pr.dma("qsp", out=ld[0:rr, LDW_ * slot:LDW_ * slot + cw],
                       in_=src[r0:r0 + rr, c0:c0 + cw], writes=[bl])
                dcol = (r0 // 128) * cols + c0
                with pr.g("dve", reads=[bl], writes=[b_dst]) as g:
                    g.tensor_copy(dst[0:rr, dcol:dcol + cw],
                                  ld[0:rr, LDW_ * slot:LDW_ * slot + cw])
                slot ^= 1

    pr.dma("qsp", out=ident[:], in_=ident_d, writes=[b_id])
    pr.dma("qsp", out=idf32[:], in_=identf_d, writes=[b_idf])
    pr.dma("qsp", out=ones[:], in_=ones_d, writes=[b_1])
    load_cast(xT, b_xT, x_fm, E, S)
    load_cast(wbuf[0], b_w[0], wqT_d, E, E)
    load_cast(wbuf[1], b_w[1], wkT_d, E, E)
    load_cast(wbuf[2], b_w[2], wvT_d, E, E)
    for (t_, b_, src) in [(biasq, b_bq, bq_d), (biask, b_bk, bk_d), (biasv, b_bv, bv_d)]:
        pr.dma("qsp", out=ld[0:1, 0:E], in_=src, writes=[b_ld[0]])
        with pr.g("dve", reads=[b_ld[0]], writes=[b_]) as g:
            g.tensor_copy(t_[:], ld[0:1, 0:E])
    load_cast(eext, b_ee, eext_d, 128, EXTW)
    load_cast(erev, b_er, erev_d, 128, EXTW)
    for t in range(NLT):
        with pr.g("dve", reads=[b_1], writes=[b_V]) as g:
            g.tensor_copy(Vsb[:, 129 * t + 64:129 * t + 65], ones[:, 0:1])

    def chunk_max(dst_col, row_sb, b_row, hb, width):
        nch = (width + 511) // 512
        for k, cc in enumerate(range(0, width, 512)):
            cw = min(512, width - cc)
            with pr.g("pe", reads=[b_row, b_1], writes=[b_kd[0]]) as g:
                g.matmul(ps_kd[0][0:1, 0:cw], ones[hb:hb + 64, 0:1],
                         row_sb[hb:hb + 64, cc:cc + cw], start=True, stop=True)
            with pr.g("dve", reads=[b_kd[0]], writes=[b_scal]) as g:
                g.tensor_reduce(scal[0:1, 56 + k:57 + k], ps_kd[0][0:1, 0:cw],
                                axis=AX, op=ALU.max)
        with pr.g("dve", reads=[b_scal], writes=[b_scal]) as g:
            g.tensor_reduce(scal[0:1, dst_col:dst_col + 1],
                            scal[0:1, 56:56 + nch], axis=AX, op=ALU.max)

    with pr.g("dve", writes=[x for sub in b_qd for x in sub]) as g:
        g.memset(qdst[:], 0.0)
    with pr.g("dve", writes=[x for sl in b_kd_sb for sub in sl for x in sub]) as g:
        g.memset(kdst[:], 0.0)
    for h in range(2):
        with pr.g("dve", writes=list(b_QPE[h])) as g:
            g.memset(QPE[h][:], 0.0)
    with pr.g("dve", writes=[bb for pair in b_KPE for bb in pair]) as g:
        g.memset(KPE[:], 0.0)
    with pr.g("dve", writes=[bb for pair in b_PT for bb in pair]) as g:
        g.memset(PT[:], 0.0)
    with pr.g("dve", writes=[b_sat]) as g:
        g.memset(satr[:], 0.0)
    with pr.g("dve", writes=[b_scal]) as g:
        g.memset(scal[:], 0.0)

    with pr.g("act", reads=[b_ee], writes=[b_qd[0][0]]) as g:
        g.activation(qdst[0:64, 0:JW], eext[0:64, PAD:PAD + JW], AF.Square)
    chunk_max(32, qdst, b_qd[0][0], 0, JW)

    def projections(p):
        f0 = 128 * p
        for (wi, bias_sb, bb, dstT, bdst) in [(0, biasq, b_bq, QT, b_QT),
                                              (1, biask, b_bk, KT, b_KT)]:
            for lc in range(0, S, 512):
                with pr.g("pe", reads=[b_w[wi], b_xT, bb, b_1], writes=[b_kd[wi]]) as g:
                    for ec in range(NEC):
                        g.matmul(ps_kd[wi][:, 0:512],
                                 wbuf[wi][:, ec * E + f0:ec * E + f0 + 128],
                                 xT[:, ec * S + lc:ec * S + lc + 512],
                                 start=(ec == 0), stop=False)
                    g.matmul(ps_kd[wi][:, 0:512], bias_sb[0:1, f0:f0 + 128],
                             ones[0:1, 0:512], start=False, stop=True)
                with pr.g("dve", reads=[b_kd[wi]], writes=[bdst]) as g:
                    g.tensor_copy(dstT[:, lc:lc + 512], ps_kd[wi][:, 0:512])
        for t in range(NLT):
            sl = t % 2
            with pr.g("pe", reads=[b_w[2], b_xT, b_bv, b_1], writes=[b_kd[sl]]) as g:
                for ec in range(NEC):
                    g.matmul(ps_kd[sl][:, 0:128],
                             xT[:, ec * S + 128 * t:ec * S + 128 * t + 128],
                             wbuf[2][:, ec * E + f0:ec * E + f0 + 128],
                             start=(ec == 0), stop=False)
                g.matmul(ps_kd[sl][:, 0:128], ones[0:1, 0:128],
                         biasv[0:1, f0:f0 + 128], start=False, stop=True)
            with pr.g("dve", reads=[b_kd[sl]], writes=[b_V]) as g:
                g.tensor_copy(Vsb[:, 129 * t + 0:129 * t + 64], ps_kd[sl][:, 0:64])
                g.tensor_copy(Vsb[:, 129 * t + 65:129 * t + 129], ps_kd[sl][:, 64:128])

    def bounds(p):
        for hh in range(2):
            hb = 64 * hh
            o = 16 * hh
            for (T, bT, col) in [(QT, b_QT, 0), (KT, b_KT, 1)]:
                with pr.g("act", reads=[bT], writes=[b_qd[0][0]]) as g:
                    g.activation(qdst[hb:hb + 64, 0:S], T[hb:hb + 64, 0:S], AF.Square)
                chunk_max(o + col, qdst, b_qd[0][0], hb, S)
            with pr.g("act", reads=[b_scal], writes=[b_scal]) as g:
                g.activation(scal[0:1, o + 2:o + 3], scal[0:1, o + 0:o + 1], AF.Ln)
                g.activation(scal[0:1, o + 3:o + 4], scal[0:1, o + 1:o + 2], AF.Ln)
                g.activation(scal[0:1, o + 4:o + 5], scal[0:1, 32:33], AF.Ln)
            with pr.g("dve", reads=[b_scal], writes=[b_scal]) as g:
                g.tensor_tensor(scal[0:1, o + 5:o + 6], scal[0:1, o + 2:o + 3],
                                scal[0:1, o + 3:o + 4], ALU.add)
                g.tensor_tensor(scal[0:1, o + 6:o + 7], scal[0:1, o + 2:o + 3],
                                scal[0:1, o + 4:o + 5], ALU.add)
                g.tensor_tensor(scal[0:1, o + 7:o + 8], scal[0:1, o + 3:o + 4],
                                scal[0:1, o + 4:o + 5], ALU.add)
            with pr.g("act", reads=[b_scal], writes=[b_scal]) as g:
                g.activation(scal[0:1, o + 8:o + 9], scal[0:1, o + 5:o + 6],
                             AF.Exp, scale=0.5)
                g.activation(scal[0:1, o + 9:o + 10], scal[0:1, o + 6:o + 7],
                             AF.Exp, scale=0.5)
                g.activation(scal[0:1, o + 10:o + 11], scal[0:1, o + 7:o + 8],
                             AF.Exp, scale=0.5)
            with pr.g("dve", reads=[b_scal], writes=[b_scal]) as g:
                g.tensor_tensor(scal[0:1, o + 11:o + 12], scal[0:1, o + 8:o + 9],
                                scal[0:1, o + 9:o + 10], ALU.add)
            with pr.g("dve", reads=[b_scal], writes=[b_scal]) as g:
                g.tensor_tensor(scal[0:1, o + 12:o + 13], scal[0:1, o + 11:o + 12],
                                scal[0:1, o + 10:o + 11], ALU.add)
            with pr.g("dve", reads=[b_scal], writes=[b_scal]) as g:
                g.tensor_scalar(scal[0:1, o + 13:o + 14], scal[0:1, o + 12:o + 13],
                                -0.125, None, ALU.mult)
            with pr.g("dve", reads=[b_scal], writes=[b_sclb]) as g:
                g.tensor_copy(scalb[0:1, hh:hh + 1], scal[0:1, o + 13:o + 14])
            with pr.g("pe", reads=[b_sclb, b_1], writes=[b_kd[0]]) as g:
                g.matmul(ps_kd[0][:, 0:1], ones[0:1, 0:128], scalb[0:1, hh:hh + 1],
                         start=True, stop=True)
            with pr.g("dve", reads=[b_kd[0]], writes=[b_bm]) as g:
                g.tensor_copy(biasm[:, hh:hh + 1], ps_kd[0][:, 0:1])

    def sat_rows(p):
        for hh in range(2):
            hb = 64 * hh
            jobs = [(bloc, ecol, T, bT)
                    for (bloc, ecol, T, bT) in [
                        (0, PAD, KT, b_KT), (1, PAD + JW - 1, KT, b_KT),
                        (2, PAD, QT, b_QT), (3, PAD + JW - 1, QT, b_QT)]
                    for cc in [None]]
            k = 0
            for (bloc, ecol, T, bT) in [(0, PAD, KT, b_KT), (1, PAD + JW - 1, KT, b_KT),
                                        (2, PAD, QT, b_QT), (3, PAD + JW - 1, QT, b_QT)]:
                for cc in range(0, S, 512):
                    pst, bt = (ps_kd[k % 2], b_kd[k % 2])
                    with pr.g("pe", reads=[b_ee, bT], writes=[bt]) as g:
                        g.matmul(pst[0:1, 0:512], eext[hb:hb + 64, ecol:ecol + 1],
                                 T[hb:hb + 64, cc:cc + 512], start=True, stop=True)
                    with pr.g("dve", reads=[bt], writes=[b_sat]) as g:
                        g.tensor_copy(
                            satr[hb:hb + 1, bloc * S + cc:bloc * S + cc + 512],
                            pst[0:1, 0:512])
                    k += 1

    def qd_prologue(p, lq):
        l_base = lq * LQ
        for i in range(NLTQ):
            l0 = l_base + 128 * i
            rts = [r0 for r0 in range(0, S, 128) if c.kind(l0, r0) == "band"]
            r_lo, r_hi = rts[0], rts[-1] + 128
            w_lo = r_lo - (l0 + 127) + MAXP - 1 + PAD
            w_hi = (r_hi - 1) - l0 + MAXP - 1 + PAD + 1
            assert 0 <= w_lo and w_hi <= EXTW, (w_lo, w_hi)
            c0 = w_lo
            for ci, cc in enumerate(range(c0, w_hi, 512)):
                cw = min(512, EXTW - cc)
                pst = ps_kd if ci % 2 == 0 else ps_qd
                bst = b_kd if ci % 2 == 0 else b_qdp
                with pr.g("pe", reads=[b_QT, b_er], writes=[bst[0], bst[1]]) as g:
                    for hh in range(2):
                        hb = 64 * hh
                        g.matmul(pst[hh][:, 0:cw], QT[hb:hb + 64, l0:l0 + 128],
                                 erev[hb:hb + 64, cc:cc + cw], start=True, stop=True)
                for hh in range(2):
                    ev = "dve" if hh == 0 else "act"
                    dst = qdst[:, EXTW * hh + cc:EXTW * hh + cc + cw]
                    with pr.g(ev, reads=[bst[hh]], writes=[b_qd[hh][ci % 6]]) as g:
                        if ev == "dve":
                            g.tensor_copy(dst, pst[hh][:, 0:cw])
                        else:
                            g.copy(dst, pst[hh][:, 0:cw])
            for hh in range(2):
                off = EXTW * hh + (r_lo - l0 + MAXP - 1 + PAD)
                src = AP(qdst.ap().tensor, off, [(2 * EXTW - 1, 128), (1, r_hi - r_lo)])
                pr.dma("qsp", out=QPE[hh][:, S * i + r_lo:S * i + r_hi], in_=src,
                       reads=list(b_qd[hh]), writes=[b_QPE[hh][i]])

    def kd_strip(p, lq, r0, slot):
        l_base = lq * LQ
        lts = [l0 for l0 in range(l_base, l_base + LQ, 128) if c.kind(l0, r0) == "band"]
        if not lts:
            return None
        l_lo, l_hi = lts[0], lts[-1] + 128
        u_lo = l_lo - (r0 + 127) + MAXP - 1 + PAD
        u_hi = (l_hi - 1) - r0 + MAXP - 1 + PAD + 1
        assert 0 <= u_lo and u_hi <= EXTW, (u_lo, u_hi)
        c0 = u_lo
        assert u_hi - c0 <= KWIN
        ksl = slot % 2
        for ci, cc in enumerate(range(c0, u_hi, 512)):
            cw = min(512, EXTW - cc)
            pst = ps_kd if ci % 2 == 0 else ps_qd
            bst = b_kd if ci % 2 == 0 else b_qdp
            with pr.g("pe", reads=[b_KT, b_ee], writes=[bst[0], bst[1]]) as g:
                for hh in range(2):
                    hb = 64 * hh
                    g.matmul(pst[hh][:, 0:cw], KT[hb:hb + 64, r0:r0 + 128],
                             eext[hb:hb + 64, cc:cc + cw], start=True, stop=True)
            for hh in range(2):
                ev = "dve" if hh == 0 else "act"
                dst = kdst[:, KWIN * (2 * ksl + hh) + (cc - c0):
                           KWIN * (2 * ksl + hh) + (cc - c0) + cw]
                with pr.g(ev, reads=[bst[hh]], writes=[b_kd_sb[ksl][hh][ci]]) as g:
                    if ev == "dve":
                        g.tensor_copy(dst, pst[hh][:, 0:cw])
                    else:
                        g.copy(dst, pst[hh][:, 0:cw])
        for hh in range(2):
            ksl = slot % 2
            off = KWIN * (2 * ksl + hh) + (l_lo - r0 + MAXP - 1 + PAD - c0)
            src = AP(kdst.ap().tensor, off, [(4 * KWIN - 1, 128), (1, l_hi - l_lo)])
            pr.dma("qsp",
                   out=KPE[:, (3 * hh + slot) * LQ:(3 * hh + slot) * LQ + (l_hi - l_lo)],
                   in_=src, reads=list(b_kd_sb[ksl][hh]), writes=[b_KPE[hh][slot]])
        return (l_lo, l_hi)

    def score_pair(p, lq, r0, kwin, kslot, slot):
        l_base = lq * LQ
        opsh = [[], []]
        reads = [b_KT, b_QT, b_id]
        for hh in range(2):
            hb = 64 * hh

            def qk(g, st, sp, hh=hh, hb=hb):
                g.matmul(ps_sc[hh][slot][:, 0:LQ], KT[hb:hb + 64, r0:r0 + 128],
                         QT[hb:hb + 64, l_base:l_base + LQ], start=st, stop=sp)
            opsh[hh].append(qk)
        for (side, kbloc, qbloc) in [("hi", 0, 2), ("lo", 1, 3)]:
            sls = [l0 for l0 in range(l_base, l_base + LQ, 128) if c.kind(l0, r0) == side]
            if not sls:
                continue
            s_lo, s_hi = sls[0] - l_base, sls[-1] + 128 - l_base
            reads += [b_sat, b_1]
            for hh in range(2):
                hb = 64 * hh

                def sat1(g, st, sp, s_lo=s_lo, s_hi=s_hi, kbloc=kbloc, hh=hh, hb=hb):
                    g.matmul(ps_sc[hh][slot][:, s_lo:s_hi],
                             satr[hb:hb + 1, kbloc * S + r0:kbloc * S + r0 + 128],
                             ones[hb:hb + 1, 0:s_hi - s_lo], start=st, stop=sp)

                def sat2(g, st, sp, s_lo=s_lo, s_hi=s_hi, qbloc=qbloc, hh=hh, hb=hb):
                    g.matmul(ps_sc[hh][slot][:, s_lo:s_hi], ones[hb:hb + 1, 0:128],
                             satr[hb:hb + 1, qbloc * S + l_base + s_lo:
                                  qbloc * S + l_base + s_hi], start=st, stop=sp)
                opsh[hh] += [sat1, sat2]
        if kwin is not None:
            l_lo, l_hi = kwin
            for l0 in range(l_lo, l_hi, 128):
                i = (l0 - l_base) // 128
                for hh in range(2):
                    def qop(g, st, sp, l0=l0, i=i, hh=hh):
                        g.matmul(ps_sc[hh][slot][:, l0 - l_base:l0 - l_base + 128],
                                 QPE[hh][:, S * i + r0:S * i + r0 + 128], ident[:],
                                 start=st, stop=sp)
                    opsh[hh].append(qop)
                reads += [b_QPE[0][i], b_QPE[1][i]]
        dedup, seen = [], set()
        for b in reads:
            if id(b) not in seen:
                seen.add(id(b))
                dedup.append(b)
        nops = len(opsh[0])
        with pr.g("pe", reads=dedup, writes=[b_sc[0][slot], b_sc[1][slot]]) as g:
            for k in range(nops):
                for hh in range(2):
                    opsh[hh][k](g, k == 0, k == nops - 1 and kwin is None)
        if kwin is not None:
            with pr.g("pe", reads=[b_KPE[0][kslot], b_KPE[1][kslot], b_id],
                      writes=[b_sc[0][slot], b_sc[1][slot]]) as g:
                for hh in range(2):
                    g.matmul(ps_sc[hh][slot][:, l_lo - l_base:l_hi - l_base], ident[:],
                             KPE[:, (3 * hh + kslot) * LQ:(3 * hh + kslot) * LQ + (l_hi - l_lo)],
                             start=False, stop=True)

    def exp_tile(p, lq, r0, hh, slot):
        with pr.g("act", reads=[b_sc[hh][slot], b_bm], writes=[b_PT[hh][slot]]) as g:
            g.activation(PT[:, (2 * hh + slot) * LQ:(2 * hh + slot) * LQ + LQ],
                         ps_sc[hh][slot][:, 0:LQ], AF.Exp,
                         bias=biasm[:, hh:hh + 1], scale=0.125)

    def pv_pair(p, lq, r0, slot, first, last):
        t = r0 // 128
        with pr.g("pe", reads=[b_V, b_PT[0][slot], b_PT[1][slot]],
                  writes=[b_ctx[0], b_ctx[1]]) as g:
            for hh in range(2):
                vcol = 129 * t + (0 if hh == 0 else 64)
                g.matmul(ps_ctx[hh][:, 0:LQ], Vsb[:, vcol:vcol + 65],
                         PT[:, (2 * hh + slot) * LQ:(2 * hh + slot) * LQ + LQ],
                         start=first, stop=last)

    def ctx_out(p, lq):
        l_base = lq * LQ
        for hh in range(2):
            h = 2 * p + hh
            with pr.g("dve", reads=[b_ctx[hh]], writes=[b_ctxsb[hh]]) as g:
                g.tensor_copy(ctxsb[:, hh * LQ:hh * LQ + LQ], ps_ctx[hh][:, 0:LQ])
            for i in range(NLTQ):
                l0 = l_base + 128 * i
                tsl = 4 * hh + i % 4
                with pr.g("pe", reads=[b_ctxsb[hh], b_idf], writes=[b_kd[hh]]) as g:
                    g.transpose(ps_kd[hh][:, 0:65],
                                ctxsb[0:65, hh * LQ + 128 * i:hh * LQ + 128 * i + 128],
                                idf32[0:65, 0:65])
                with pr.g("dve", reads=[b_kd[hh]], writes=[b_trp[tsl]]) as g:
                    g.tensor_copy(trp[:, 65 * tsl:65 * tsl + 65], ps_kd[hh][:, 0:65])
                pr.dma("qsp", out=ctx_dram[l0:l0 + 128, 65 * h:65 * h + 65],
                       in_=trp[:, 65 * tsl:65 * tsl + 65],
                       reads=[b_trp[tsl], b_ctxd[l0 // 128]])

    for p in range(c.NPAIR if stages >= 90 else (1 if stages >= 2 else 0)):
        projections(p)
        if stages >= 3:
            bounds(p)
        if stages >= 4:
            sat_rows(p)
        if p == 0:
            dump("QT0", QT[:], (128, S), BF16, [b_QT])
            dump("KT0", KT[:], (128, S), BF16, [b_KT])
            dump("Vsb", Vsb[:], (128, NLT * 129), BF16, [b_V])
            dump("satr", satr[:], (128, 4 * S), BF16, [b_sat])
            dump("biasm", biasm[:], (128, 2), F32, [b_bm])
        nlq_eff = NLQ if stages >= 80 else (1 if stages >= 5 else 0)
        for lq in range(nlq_eff):
            qd_prologue(p, lq)
            if p == 0 and lq == 0:
                dump("QPE0", QPE[0][:], (128, NLTQ * S), BF16, b_QPE[0])
            rmax = S if stages >= 80 else (128 if stages >= 6 else 0)
            l_mid = lq * LQ + LQ // 2
            r0s = sorted(range(0, rmax, 128), key=lambda r: -abs(r + 64 - l_mid))
            kwins = {}
            if rmax:
                kwins[r0s[0]] = kd_strip(p, lq, r0s[0], 0)
            if len(r0s) > 1:
                kwins[r0s[1]] = kd_strip(p, lq, r0s[1], 1)
            for ri, r0 in enumerate(r0s):
                slot = ri % 2
                if p == 0 and lq == 0 and r0 == 0:
                    dump("KPE0", KPE[:], (128, 6 * LQ), BF16,
                         [bb for pair in b_KPE for bb in pair])
                if stages >= 7:
                    score_pair(p, lq, r0, kwins[r0], ri % 3, slot)
                    if stages >= 71:
                        for hh in range(2):
                            exp_tile(p, lq, r0, hh, slot)
                if ri + 2 < len(r0s):
                    kwins[r0s[ri + 2]] = kd_strip(p, lq, r0s[ri + 2], (ri + 2) % 3)
                if stages >= 72:
                    pv_pair(p, lq, r0, slot, ri == 0, ri == len(r0s) - 1)
                if p == 0 and lq == 0 and r0 == 0:
                    dump("PT0", PT[:], (128, 4 * LQ), BF16,
                         [bb for pair in b_PT for bb in pair])
            if stages >= 80:
                ctx_out(p, lq)

    es2.close()

    resid = sb("resid", [128, 2 * E], F32, es=es3)
    b_res2 = [Buf(), Buf()]
    sqf = sb("sqf", [128, 2 * E], F32, es=es3)
    b_sqf2 = [Buf(), Buf()]
    ctxt = sb("ctxt", [128, 2 * c.H * 65], F32, es=es3)
    b_ct2 = [Buf(), Buf()]
    xt_sb = sb("xt_sb", [128, 4 * E], F32, es=es3)
    b_xts2 = [Buf() for _ in range(4)]
    ln1bf = sb("ln1bf", [128, 2 * E], es=es3)
    b_l1b2 = [Buf(), Buf()]
    ln1T = sb("ln1T", [128, NEC * S], es=es3)
    b_l1T = [Buf() for _ in range(NLT)]
    ps_tb = psum("ps_tb", [128, 2048], BF16, es=es3)
    b_tb = [Buf(), Buf()]

    def ln_stats(b_src, src_sb, sl=0):
        b_st = b_st2[sl]
        so = 16 * sl
        nh = (E + 511) // 512
        with pr.g("dve", reads=[b_src], writes=[b_st]) as g:
            for hhh in range(nh):
                w = min(512, E - 512 * hhh)
                g.bn_stats(stat[:, so + 6 * hhh:so + 6 * hhh + 6],
                           src_sb[:, 512 * hhh:512 * hhh + w])
        with pr.g("dve", reads=[b_st], writes=[b_st]) as g:
            g.bn_aggr(stat[:, so + 12:so + 14], stat[:, so:so + 6 * nh])
        with pr.g("dve", reads=[b_st], writes=[b_st]) as g:
            g.tensor_scalar(stat[:, so + 14:so + 15], stat[:, so + 13:so + 14], 1.0,
                            1e-5, ALU.mult, ALU.add)
        with pr.g("act", reads=[b_st], writes=[b_st]) as g:
            g.activation(stat[:, so + 15:so + 16], stat[:, so + 14:so + 15], AF.Ln)
        with pr.g("act", reads=[b_st], writes=[b_st]) as g:
            g.activation(stat[:, so + 10:so + 11], stat[:, so + 15:so + 16],
                         AF.Exp, scale=-0.5)
        with pr.g("dve", reads=[b_src, b_st], writes=[b_src]) as g:
            g.tensor_scalar(src_sb, src_sb, stat[:, so + 12:so + 13],
                            stat[:, so + 10:so + 11], ALU.subtract, ALU.mult)

    HC = c.H * 65

    def ln1_tile(t):
        l0 = 128 * t
        sl = t % 2
        b_xts, b_ct = b_xts2[sl], b_ct2[sl]
        b_res = b_res2[sl]
        rs_v = resid[:, sl * E:sl * E + E]
        pr.dma("qsp", out=xt_sb[:, sl * E:sl * E + E], in_=x_tok[l0:l0 + 128, 0:E],
               writes=[b_xts])
        pr.dma("qsp", out=ctxt[:, sl * HC:sl * HC + HC],
               in_=ctx_dram[l0:l0 + 128, 0:HC], writes=[b_ct, b_ctxd[t]])
        with pr.g("dve", reads=[b_ct], writes=[b_ct]) as g:
            for h in range(c.H):
                zc = sl * HC + 65 * h + (64 if h % 2 == 0 else 0)
                g.reciprocal(ctxt[:, zc:zc + 1], ctxt[:, zc:zc + 1])
        with pr.g("dve", reads=[b_ct], writes=[b_res]) as g:
            for h in range(c.H):
                zc = sl * HC + 65 * h + (64 if h % 2 == 0 else 0)
                cc = sl * HC + 65 * h + (0 if h % 2 == 0 else 1)
                g.tensor_scalar(rs_v[:, 64 * h:64 * h + 64], ctxt[:, cc:cc + 64],
                                ctxt[:, zc:zc + 1], None, ALU.mult)
        with pr.g("dve", reads=[b_xts, b_res], writes=[b_res]) as g:
            g.tensor_tensor(rs_v, rs_v, xt_sb[:, sl * E:sl * E + E], ALU.add)
        ln_stats(b_res, rs_v, sl)
        pr.dma("qsp", out=ln1_dram[l0:l0 + 128, 0:E], in_=rs_v, reads=[b_res],
               writes=[b_ln1d[t]])
        with pr.g("dve", reads=[b_res], writes=[b_l1b2[sl]]) as g:
            g.tensor_copy(ln1bf[:, sl * E:sl * E + E], rs_v)
        for ec in range(NEC):
            esl = ec % 2
            with pr.g("pe", reads=[b_l1b2[sl], b_id], writes=[b_tb[esl]]) as g:
                g.transpose(ps_tb[:, 1024 * esl:1024 * esl + 128],
                            ln1bf[:, sl * E + 128 * ec:sl * E + 128 * ec + 128], ident[:])
            with pr.g("dve", reads=[b_tb[esl]], writes=[b_l1T[t]]) as g:
                g.tensor_copy(ln1T[:, ec * S + l0:ec * S + l0 + 128],
                              ps_tb[:, 1024 * esl:1024 * esl + 128])
        if t == 0:
            dump("ln1_0", resid[:, 0:E], (128, E), F32, [b_res2[0]])

    load_cast(wbuf[0], b_w[0], wr_d, E, E)
    load_cast(wbuf[1], b_w[1], wkf_d, E, E)
    load_cast(wbuf[2], b_w[2], wvf_d, E, E)
    mur = sb("mur_sb", [128, NEC], F32, es=es3); b_mur = Buf()
    muk = sb("muk_sb", [128, NEC], F32, es=es3); b_muk = Buf()
    murc = sb("murc", [128, 2 * NEC], F32, es=es3); b_murc = Buf()
    for (t_, b_, src) in [(mur, b_mur, mur_d), (muk, b_muk, muk_d)]:
        pr.dma("qsp", out=ld[0:128, 0:NEC], in_=src, writes=[b_ld[0]])
        with pr.g("dve", reads=[b_ld[0]], writes=[b_]) as g:
            g.tensor_copy(t_[:], ld[0:128, 0:NEC])
    with pr.g("dve", reads=[b_mur, b_muk], writes=[b_murc]) as g:
        g.tensor_scalar(murc[:, 0:NEC], mur[:], -1.0, 1.0, ALU.mult, ALU.add)
        g.tensor_scalar(murc[:, NEC:2 * NEC], muk[:], -1.0, 1.0, ALU.mult, ALU.add)

    LB = LQ
    xrT = sb("xrT", [128, NEC * LB], es=es3); b_xr = Buf()
    xkT = sb("xkT", [128, NEC * LB], es=es3); b_xk = Buf()
    rsb = sb("rsb", [128, NEC * LB], F32, es=es3); b_r = Buf()
    kact = sb("kact", [128, NEC * LB], es=es3); b_ka = Buf()
    ffnT = sb("ffnT", [128, NEC * LB], F32, es=es3); b_ff = Buf()
    otile = sb("otile", [128, (512 // 128) * E], F32, es=es3); b_ot = Buf()

    def ffn_block(bl):
        l0 = bl * LB
        t_lo = max(0, bl * LB // 128 - 1)
        t_hi = min(NLT, (bl * LB + LB) // 128)
        l1deps = [b_l1T[tt] for tt in range(t_lo, t_hi)]
        with pr.g("dve", reads=l1deps + [b_murc], writes=[b_xr, b_xk]) as g:
            for ec in range(NEC):
                if l0 == 0:
                    g.memset(xrT[:, ec * LB:ec * LB + 1], 0.0)
                    g.memset(xkT[:, ec * LB:ec * LB + 1], 0.0)
                    d0, sh = 1, 0
                else:
                    d0, sh = 0, -1
                g.tensor_scalar(xrT[:, ec * LB + d0:ec * LB + LB],
                                ln1T[:, ec * S + l0 + sh:ec * S + l0 + LB - 1],
                                murc[:, ec:ec + 1], None, ALU.mult)
                g.tensor_scalar(xkT[:, ec * LB + d0:ec * LB + LB],
                                ln1T[:, ec * S + l0 + sh:ec * S + l0 + LB - 1],
                                murc[:, NEC + ec:NEC + ec + 1], None, ALU.mult)
        with pr.g("dve", reads=l1deps + [b_mur, b_muk, b_xr, b_xk],
                  writes=[b_xr, b_xk]) as g:
            for ec in range(NEC):
                g.scalar_tensor_tensor(xrT[:, ec * LB:ec * LB + LB],
                                       ln1T[:, ec * S + l0:ec * S + l0 + LB],
                                       mur[:, ec:ec + 1], xrT[:, ec * LB:ec * LB + LB],
                                       ALU.mult, ALU.add)
                g.scalar_tensor_tensor(xkT[:, ec * LB:ec * LB + LB],
                                       ln1T[:, ec * S + l0:ec * S + l0 + LB],
                                       muk[:, ec:ec + 1], xkT[:, ec * LB:ec * LB + LB],
                                       ALU.mult, ALU.add)
        for dc in range(NEC):
            pr1, br1 = (ps_kd[0], b_kd[0]) if dc % 2 == 0 else (ps_qd[0], b_qdp[0])
            pr2, br2 = (ps_kd[1], b_kd[1]) if dc % 2 == 0 else (ps_qd[1], b_qdp[1])
            with pr.g("pe", reads=[b_w[0], b_xr], writes=[br1]) as g:
                for ec in range(NEC):
                    g.matmul(pr1[:, 0:LB],
                             wbuf[0][:, ec * E + 128 * dc:ec * E + 128 * dc + 128],
                             xrT[:, ec * LB:ec * LB + LB],
                             start=(ec == 0), stop=(ec == NEC - 1))
            with pr.g("act", reads=[br1], writes=[b_r]) as g:
                g.activation(rsb[:, dc * LB:dc * LB + LB], pr1[:, 0:LB], AF.Sigmoid)
            with pr.g("pe", reads=[b_w[1], b_xk], writes=[br2]) as g:
                for ec in range(NEC):
                    g.matmul(pr2[:, 0:LB],
                             wbuf[1][:, ec * E + 128 * dc:ec * E + 128 * dc + 128],
                             xkT[:, ec * LB:ec * LB + LB],
                             start=(ec == 0), stop=(ec == NEC - 1))
            with pr.g("act", reads=[br2], writes=[b_ka]) as g:
                g.activation(kact[:, dc * LB:dc * LB + LB], pr2[:, 0:LB], AF.Relu)
            with pr.g("act", reads=[b_ka], writes=[b_ka]) as g:
                g.activation(kact[:, dc * LB:dc * LB + LB],
                             kact[:, dc * LB:dc * LB + LB], AF.Square)
        NSUB = LB // 128
        for i in range(NSUB):
            lt = l0 + 128 * i
            pr.dma("qsp", out=xt_sb[:, i * E:i * E + E],
                   in_=ln1_dram[lt:lt + 128, 0:E], reads=[b_ln1d[lt // 128]],
                   writes=[b_xts2[i]])
        b_ffc = [Buf() for _ in range(NEC)]
        for dc in range(NEC):
            psv, bsv = (ps_sc[0][1], b_sc[0][1]) if dc % 2 == 0 else (ps_sc[1][1], b_sc[1][1])
            with pr.g("pe", reads=[b_w[2], b_ka], writes=[bsv]) as g:
                for ec in range(NEC):
                    g.matmul(psv[:, 0:LB],
                             wbuf[2][:, ec * E + 128 * dc:ec * E + 128 * dc + 128],
                             kact[:, ec * LB:ec * LB + LB],
                             start=(ec == 0), stop=(ec == NEC - 1))
            with pr.g("dve", reads=[bsv, b_r], writes=[b_ffc[dc]]) as g:
                g.tensor_tensor(ffnT[:, dc * LB:dc * LB + LB], psv[:, 0:LB],
                                rsb[:, dc * LB:dc * LB + LB], ALU.mult)
            for i in range(NSUB):
                sl = (dc + i) % 2
                with pr.g("pe", reads=[b_ffc[dc], b_idf], writes=[b_kd[sl]]) as g:
                    g.transpose(ps_kd[sl][:, 0:128],
                                ffnT[0:128, dc * LB + 128 * i:dc * LB + 128 * i + 128],
                                idf32[:])
                with pr.g("dve", reads=[b_kd[sl], b_xts2[i]], writes=[b_ot]) as g:
                    g.tensor_tensor(otile[:, i * E + 128 * dc:i * E + 128 * dc + 128],
                                    ps_kd[sl][:, 0:128],
                                    xt_sb[:, i * E + 128 * dc:i * E + 128 * dc + 128],
                                    ALU.add)
        for i in range(NSUB):
            lt = l0 + 128 * i
            ln_stats(b_ot, otile[:, i * E:i * E + E], i % 2)
            pr.dma("qsp", out=out_t[lt:lt + 128, 0:E], in_=otile[:, i * E:i * E + E],
                   reads=[b_ot])
        if bl == 0:
            dump("xrT0", xrT[:], (128, NEC * LB), BF16, [b_xr])
            dump("ffnT0", ffnT[:], (128, NEC * LB), F32, b_ffc)

    TPB = LB // 128
    if stages >= 100:
        for t in range(NLT):
            ln1_tile(t)
            if stages >= 110 and (t + 1) % TPB == 0:
                ffn_block((t + 1) // TPB - 1)

    for i in range(4):
        if pr.cnt[f"qsp{i}"]:
            nc.sync.wait_ge(pr.sem[f"qsp{i}"], pr.cnt[f"qsp{i}"])
        if pr.cnt[f"qact{i}"]:
            nc.sync.wait_ge(pr.sem[f"qact{i}"], pr.cnt[f"qact{i}"])
    es3.close()
    pr.es.close()
    return nc, dbg_used


def host_inputs(cfg, x_b, wq, bq, wk, bk, wv, bv, dist, W_r, W_k, W_v, mu_r, mu_k):
    c = cfg
    ar = lambda a: np.ascontiguousarray(a, np.float32)
    idx = np.arange(c.EXTW) - c.PAD
    e_ext = dist[np.clip(idx, 0, c.JW - 1)]
    e_rev = dist[np.clip(c.JW - 1 - idx, 0, c.JW - 1)]
    em = {
        "x_tok": ar(x_b), "x_fm": ar(x_b.T),
        "wqT": ar(wq.T), "wkT": ar(wk.T), "wvT": ar(wv.T),
        "bqv": ar(bq.reshape(1, -1)), "bkv": ar(bk.reshape(1, -1)),
        "bvv": ar(bv.reshape(1, -1)),
        "e_ext": ar(np.concatenate([e_ext.T, e_ext.T], axis=0)),
        "e_rev": ar(np.concatenate([e_rev.T, e_rev.T], axis=0)),
        "wrT": ar(W_r.T), "wkfT": ar(W_k.T), "wvfT": ar(W_v.T),
        "mur": ar(mu_r.reshape(c.NEC, 128).T),
        "muk": ar(mu_k.reshape(c.NEC, 128).T),
        "ident": np.eye(128, dtype=BF),
        "identf": np.eye(128, dtype=np.float32),
        "ones_bf": np.ones((128, 512), dtype=BF),
    }
    return em


def kernel(L0_attn_hidden_states, L0_attn_wq, L0_attn_bq, L0_attn_wk, L0_attn_bk,
           L0_attn_wv, L0_attn_bv, L0_attn_dist_emb, L0_ffn_W_r, L0_ffn_W_k,
           L0_ffn_W_v, L0_ffn_mu_r, L0_ffn_mu_k):
    cfg = Cfg(S=2048, E=1024, MAXP=1024)
    f = lambda a: np.asarray(a, np.float32)
    x = np.ascontiguousarray(f(L0_attn_hidden_states))
    nc, _ = build_program(cfg)
    in_maps = [host_inputs(cfg, x[b], f(L0_attn_wq), f(L0_attn_bq), f(L0_attn_wk),
                           f(L0_attn_bk), f(L0_attn_wv), f(L0_attn_bv),
                           f(L0_attn_dist_emb), f(L0_ffn_W_r), f(L0_ffn_W_k),
                           f(L0_ffn_W_v), f(L0_ffn_mu_r), f(L0_ffn_mu_k))
               for b in range(8)]
    res = run_bass_kernel_spmd(nc, in_maps, core_ids=list(range(8)))
    return np.stack([np.asarray(res.results[b]["out"], np.float32) for b in range(8)])


# version 26
# speedup vs baseline: 1.3037x; 1.0021x over previous
import os
import numpy as np
import ml_dtypes
import concourse.bass as bass
import concourse.mybir as mybir
from concourse.ap import AP
from concourse.bass_utils import run_bass_kernel_spmd
from contextlib import ExitStack

F32 = mybir.dt.float32
BF16 = mybir.dt.bfloat16
AF = mybir.ActivationFunctionType
ALU = mybir.AluOpType
BF = ml_dtypes.bfloat16
AX = mybir.AxisListType.X


class Cfg:
    def __init__(self, S=2048, E=1024, MAXP=1024):
        self.S, self.E, self.MAXP = S, E, MAXP
        self.D = 64
        self.H = E // 64
        self.NPAIR = self.H // 2
        self.JW = 2 * MAXP - 1
        self.PAD = 255
        self.EXTW = self.JW + 2 * self.PAD
        self.NLT = S // 128
        self.LQ = min(512, S)
        self.NLQ = S // self.LQ
        self.NLTQ = self.LQ // 128
        self.NEC = E // 128
        self.SAT = MAXP + 128

    def kind(self, l0, r0):
        if l0 - r0 >= self.SAT:
            return "lo"
        if r0 - l0 >= self.SAT:
            return "hi"
        return "band"


class Buf:
    __slots__ = ("writer", "readers")

    def __init__(self):
        self.writer = None
        self.readers = []


def build_program(cfg, dbg=(), stages=999):
    c = cfg
    S, E, NEC = c.S, c.E, c.NEC
    MAXP, JW, PAD, EXTW = c.MAXP, c.JW, c.PAD, c.EXTW
    LQ, NLQ, NLTQ, NLT = c.LQ, c.NLQ, c.NLTQ, c.NLT
    KWIN = ((LQ + 255 + 511) // 512 + 1) * 512
    nc = bass.Bass("TRN2", target_bir_lowering=False, debug=False)

    def din(name, shape, dt=F32):
        return nc.dram_tensor(name, shape, dt, kind="ExternalInput").ap()

    x_tok = din("x_tok", [S, E])
    x_fm = din("x_fm", [E, S])
    wqT_d = din("wqT", [E, E]); wkT_d = din("wkT", [E, E]); wvT_d = din("wvT", [E, E])
    bq_d = din("bqv", [1, E]); bk_d = din("bkv", [1, E]); bv_d = din("bvv", [1, E])
    eext_d = din("e_ext", [128, EXTW])
    erev_d = din("e_rev", [128, EXTW])
    wr_d = din("wrT", [E, E]); wkf_d = din("wkfT", [E, E]); wvf_d = din("wvfT", [E, E])
    mur_d = din("mur", [128, NEC]); muk_d = din("muk", [128, NEC])
    ident_d = din("ident", [128, 128], BF16)
    identf_d = din("identf", [128, 128], F32)
    ones_d = din("ones_bf", [128, 512], BF16)
    out_t = nc.dram_tensor("out", [S, E], F32, kind="ExternalOutput").ap()
    ctx_dram = nc.dram_tensor("ctx_scr", [S, c.H * 65], F32, kind="Internal").ap()
    ln1_dram = nc.dram_tensor("ln1_scr", [S, E], F32, kind="Internal").ap()
    dbg_used = []

    NS = 8

    class Prog:
        def __init__(self):
            self.eng = {"pe": nc.tensor, "dve": nc.vector, "act": nc.scalar,
                        "sp": nc.sync}
            self.es = ExitStack()
            names = (["pe", "dve", "act", "sp"]
                     + [f"qsp{i}" for i in range(NS)] + [f"qact{i}" for i in range(NS)])
            self.sem = {k: self.es.enter_context(nc.semaphore("sem_" + k))
                        for k in names}
            self.cnt = {k: 0 for k in self.sem}
            self.qtot = {"qsp": 0, "qact": 0}
            self.waited = {}
            self.issuer = {"qsp": "sp", "qact": "act"}

        def _wait(self, eng, dep, val):
            assert val <= self.cnt[dep], f"future wait {eng}<-{dep}@{val}"
            if self.waited.get((eng, dep), -1) >= val:
                return
            self.eng[eng].wait_ge(self.sem[dep], val)
            self.waited[(eng, dep)] = val

        def _pre(self, eng, reads, writes):
            deps = {}

            def add(e, cc):
                if deps.get(e, 0) < cc:
                    deps[e] = cc
            for b in reads:
                if b.writer:
                    add(*b.writer)
            for b in writes:
                if b.writer:
                    add(*b.writer)
                for rc in b.readers:
                    add(*rc)
            for e in sorted(deps):
                if e == eng == "pe":
                    continue
                self._wait(eng, e, deps[e])

        def g(self, eng, reads=(), writes=()):
            pr = self

            class G:
                def __init__(gg):
                    gg.last = None

                def __getattr__(gg, item):
                    fn = getattr(pr.eng[eng], item)

                    def wrap(*a, **k):
                        inst = fn(*a, **k)
                        gg.last = inst
                        return inst
                    return wrap

            class Ctx:
                def __enter__(ctx):
                    pr._pre(eng, reads, writes)
                    ctx.grp = G()
                    return ctx.grp

                def __exit__(ctx, et, ev, tb):
                    if et is not None:
                        return False
                    assert ctx.grp.last is not None
                    ctx.grp.last.then_inc(pr.sem[eng], 1)
                    pr.cnt[eng] += 1
                    mark = (eng, pr.cnt[eng])
                    for b in writes:
                        b.writer = mark
                        b.readers = []
                    for b in reads:
                        b.readers.append(mark)
                    return False
            return Ctx()

        def dma(self, q, out, in_, reads=(), writes=()):
            eng = self.issuer[q]
            s = f"{q}{self.qtot[q] % NS}"
            self.qtot[q] += 1
            self._pre(eng, reads, writes)
            if self.cnt[s] > 0:
                self._wait(eng, s, self.cnt[s])
            self.eng[eng].dma_start(out=out, in_=in_).then_inc(self.sem[s], 16)
            self.cnt[s] += 16
            mark = (s, self.cnt[s])
            for b in writes:
                b.writer = mark
                b.readers = []
            for b in reads:
                b.readers.append(mark)

    pr = Prog()
    b_ctxd = [Buf() for _ in range(NLT)]
    b_ln1d = [Buf() for _ in range(NLT)]
    es2 = ExitStack()
    es3 = ExitStack()

    def sb(n, shape, dt=BF16, es=None):
        return (es or pr.es).enter_context(nc.sbuf_tensor(n, shape, dt))

    def psum(n, shape, dt=F32, es=None):
        return (es or pr.es).enter_context(nc.psum_tensor(n, shape, dt))

    def dump(name, src, shape, dt, reads):
        if name not in dbg:
            return
        o = nc.dram_tensor("dbg_" + name, list(shape), dt, kind="ExternalOutput").ap()
        dbg_used.append(name)
        pr.dma("qsp", out=o, in_=src, reads=list(reads))

    LDW_ = (max(S, E) + 511) // 512 * 512 // 2
    ld = sb("ld", [128, 2 * LDW_], F32)
    b_ld = [Buf(), Buf()]
    wbuf = [sb(f"wbuf{i}", [128, NEC * E]) for i in range(3)]
    b_w = [Buf() for _ in range(3)]
    ident = sb("identity", [128, 128]); b_id = Buf()
    idf32 = sb("idf32", [128, 128], F32); b_idf = Buf()
    ones = sb("ones", [128, 512]); b_1 = Buf()
    stat = sb("stat", [128, 32], F32)
    b_st2 = [Buf(), Buf()]

    xT = sb("xT", [128, NEC * S], es=es2); b_xT = Buf()
    eext = sb("eext", [128, EXTW], es=es2); b_ee = Buf()
    erev = sb("erev", [128, EXTW], es=es2); b_er = Buf()
    biasq = sb("biasq", [1, E], es=es2); b_bq = Buf()
    biask = sb("biask", [1, E], es=es2); b_bk = Buf()
    biasv = sb("biasv", [1, E], es=es2); b_bv = Buf()
    QT = sb("QT", [128, S], es=es2); b_QT = Buf()
    KT = sb("KT", [128, S], es=es2); b_KT = Buf()
    Vsb = sb("Vsb", [128, NLT * 129], es=es2); b_V = Buf()
    scal = sb("scal", [1, 64], F32, es=es2); b_scal = Buf()
    scalb = sb("scalb", [1, 4], es=es2); b_sclb = Buf()
    biasm = sb("biasm", [128, 2], F32, es=es2); b_bm = Buf()
    satr = sb("satr", [128, 4 * S], es=es2); b_sat = Buf()
    qdst = sb("qdst", [128, 2 * EXTW], es=es2)
    b_qd = [[Buf() for _ in range(6)] for _ in range(2)]
    QPE = [sb(f"QPE{h}", [128, NLTQ * S], es=es2) for h in range(2)]
    b_QPE = [[Buf() for _ in range(NLTQ)] for _ in range(2)]
    kdst = sb("kdst", [128, 4 * KWIN], es=es2)
    b_kd_sb = [[[Buf() for _ in range(3)] for _ in range(2)] for _ in range(2)]
    KPE = sb("KPE", [128, 6 * LQ], es=es2)
    b_KPE = [[Buf(), Buf(), Buf()], [Buf(), Buf(), Buf()]]
    PT = sb("PT", [128, 4 * LQ], es=es2); b_PT = [[Buf(), Buf()], [Buf(), Buf()]]
    ctxsb = sb("ctxsb", [65, 2 * LQ], F32, es=es2); b_ctxsb = [Buf(), Buf()]
    trp = sb("trp", [128, 8 * 65], F32, es=es2)
    b_trp = [Buf() for _ in range(8)]

    ps_kd = [psum(f"ps_kd{h}", [128, 512]) for h in range(2)]
    b_kd = [Buf(), Buf()]
    ps_qd = [psum(f"ps_qd{h}", [128, 512]) for h in range(2)]
    b_qdp = [Buf(), Buf()]
    _pssc = [psum(f"ps_sc{h}", [128, LQ]) for h in range(2)]
    _bsc = [Buf(), Buf()]
    ps_sc = [[_pssc[h], _pssc[h]] for h in range(2)]
    b_sc = [[_bsc[h], _bsc[h]] for h in range(2)]
    ps_ctx = [psum(f"ps_ctx{h}", [65, LQ], es=es2) for h in range(2)]
    b_ctx = [Buf(), Buf()]

    def load_cast(dst, b_dst, src, rows, cols):
        slot = 0
        for r0 in range(0, rows, 128):
            rr = min(128, rows - r0)
            for c0 in range(0, cols, LDW_):
                cw = min(LDW_, cols - c0)
                bl = b_ld[slot]
                pr.dma("qsp", out=ld[0:rr, LDW_ * slot:LDW_ * slot + cw],
                       in_=src[r0:r0 + rr, c0:c0 + cw], writes=[bl])
                dcol = (r0 // 128) * cols + c0
                with pr.g("dve", reads=[bl], writes=[b_dst]) as g:
                    g.tensor_copy(dst[0:rr, dcol:dcol + cw],
                                  ld[0:rr, LDW_ * slot:LDW_ * slot + cw])
                slot ^= 1

    pr.dma("qsp", out=ident[:], in_=ident_d, writes=[b_id])
    pr.dma("qsp", out=idf32[:], in_=identf_d, writes=[b_idf])
    pr.dma("qsp", out=ones[:], in_=ones_d, writes=[b_1])
    load_cast(xT, b_xT, x_fm, E, S)
    load_cast(wbuf[0], b_w[0], wqT_d, E, E)
    load_cast(wbuf[1], b_w[1], wkT_d, E, E)
    load_cast(wbuf[2], b_w[2], wvT_d, E, E)
    for (t_, b_, src) in [(biasq, b_bq, bq_d), (biask, b_bk, bk_d), (biasv, b_bv, bv_d)]:
        pr.dma("qsp", out=ld[0:1, 0:E], in_=src, writes=[b_ld[0]])
        with pr.g("dve", reads=[b_ld[0]], writes=[b_]) as g:
            g.tensor_copy(t_[:], ld[0:1, 0:E])
    load_cast(eext, b_ee, eext_d, 128, EXTW)
    load_cast(erev, b_er, erev_d, 128, EXTW)
    for t in range(NLT):
        with pr.g("dve", reads=[b_1], writes=[b_V]) as g:
            g.tensor_copy(Vsb[:, 129 * t + 64:129 * t + 65], ones[:, 0:1])

    def chunk_max(dst_col, row_sb, b_row, hb, width):
        nch = (width + 511) // 512
        for k, cc in enumerate(range(0, width, 512)):
            cw = min(512, width - cc)
            with pr.g("pe", reads=[b_row, b_1], writes=[b_kd[0]]) as g:
                g.matmul(ps_kd[0][0:1, 0:cw], ones[hb:hb + 64, 0:1],
                         row_sb[hb:hb + 64, cc:cc + cw], start=True, stop=True)
            with pr.g("dve", reads=[b_kd[0]], writes=[b_scal]) as g:
                g.tensor_reduce(scal[0:1, 56 + k:57 + k], ps_kd[0][0:1, 0:cw],
                                axis=AX, op=ALU.max)
        with pr.g("dve", reads=[b_scal], writes=[b_scal]) as g:
            g.tensor_reduce(scal[0:1, dst_col:dst_col + 1],
                            scal[0:1, 56:56 + nch], axis=AX, op=ALU.max)

    with pr.g("dve", writes=[x for sub in b_qd for x in sub]) as g:
        g.memset(qdst[:], 0.0)
    with pr.g("dve", writes=[x for sl in b_kd_sb for sub in sl for x in sub]) as g:
        g.memset(kdst[:], 0.0)
    for h in range(2):
        with pr.g("dve", writes=list(b_QPE[h])) as g:
            g.memset(QPE[h][:], 0.0)
    with pr.g("dve", writes=[bb for pair in b_KPE for bb in pair]) as g:
        g.memset(KPE[:], 0.0)
    with pr.g("dve", writes=[bb for pair in b_PT for bb in pair]) as g:
        g.memset(PT[:], 0.0)
    with pr.g("dve", writes=[b_sat]) as g:
        g.memset(satr[:], 0.0)
    with pr.g("dve", writes=[b_scal]) as g:
        g.memset(scal[:], 0.0)

    with pr.g("act", reads=[b_ee], writes=[b_qd[0][0]]) as g:
        g.activation(qdst[0:64, 0:JW], eext[0:64, PAD:PAD + JW], AF.Square)
    chunk_max(32, qdst, b_qd[0][0], 0, JW)

    def projections(p):
        f0 = 128 * p
        for (wi, bias_sb, bb, dstT, bdst) in [(0, biasq, b_bq, QT, b_QT),
                                              (1, biask, b_bk, KT, b_KT)]:
            for lc in range(0, S, 512):
                with pr.g("pe", reads=[b_w[wi], b_xT, bb, b_1], writes=[b_kd[wi]]) as g:
                    for ec in range(NEC):
                        g.matmul(ps_kd[wi][:, 0:512],
                                 wbuf[wi][:, ec * E + f0:ec * E + f0 + 128],
                                 xT[:, ec * S + lc:ec * S + lc + 512],
                                 start=(ec == 0), stop=False)
                    g.matmul(ps_kd[wi][:, 0:512], bias_sb[0:1, f0:f0 + 128],
                             ones[0:1, 0:512], start=False, stop=True)
                with pr.g("dve", reads=[b_kd[wi]], writes=[bdst]) as g:
                    g.tensor_copy(dstT[:, lc:lc + 512], ps_kd[wi][:, 0:512])
        for t in range(NLT):
            sl = t % 2
            with pr.g("pe", reads=[b_w[2], b_xT, b_bv, b_1], writes=[b_kd[sl]]) as g:
                for ec in range(NEC):
                    g.matmul(ps_kd[sl][:, 0:128],
                             xT[:, ec * S + 128 * t:ec * S + 128 * t + 128],
                             wbuf[2][:, ec * E + f0:ec * E + f0 + 128],
                             start=(ec == 0), stop=False)
                g.matmul(ps_kd[sl][:, 0:128], ones[0:1, 0:128],
                         biasv[0:1, f0:f0 + 128], start=False, stop=True)
            with pr.g("dve", reads=[b_kd[sl]], writes=[b_V]) as g:
                g.tensor_copy(Vsb[:, 129 * t + 0:129 * t + 64], ps_kd[sl][:, 0:64])
                g.tensor_copy(Vsb[:, 129 * t + 65:129 * t + 129], ps_kd[sl][:, 64:128])

    def bounds(p):
        for hh in range(2):
            hb = 64 * hh
            o = 16 * hh
            for (T, bT, col) in [(QT, b_QT, 0), (KT, b_KT, 1)]:
                with pr.g("act", reads=[bT], writes=[b_qd[0][0]]) as g:
                    g.activation(qdst[hb:hb + 64, 0:S], T[hb:hb + 64, 0:S], AF.Square)
                chunk_max(o + col, qdst, b_qd[0][0], hb, S)
            with pr.g("act", reads=[b_scal], writes=[b_scal]) as g:
                g.activation(scal[0:1, o + 2:o + 3], scal[0:1, o + 0:o + 1], AF.Ln)
                g.activation(scal[0:1, o + 3:o + 4], scal[0:1, o + 1:o + 2], AF.Ln)
                g.activation(scal[0:1, o + 4:o + 5], scal[0:1, 32:33], AF.Ln)
            with pr.g("dve", reads=[b_scal], writes=[b_scal]) as g:
                g.tensor_tensor(scal[0:1, o + 5:o + 6], scal[0:1, o + 2:o + 3],
                                scal[0:1, o + 3:o + 4], ALU.add)
                g.tensor_tensor(scal[0:1, o + 6:o + 7], scal[0:1, o + 2:o + 3],
                                scal[0:1, o + 4:o + 5], ALU.add)
                g.tensor_tensor(scal[0:1, o + 7:o + 8], scal[0:1, o + 3:o + 4],
                                scal[0:1, o + 4:o + 5], ALU.add)
            with pr.g("act", reads=[b_scal], writes=[b_scal]) as g:
                g.activation(scal[0:1, o + 8:o + 9], scal[0:1, o + 5:o + 6],
                             AF.Exp, scale=0.5)
                g.activation(scal[0:1, o + 9:o + 10], scal[0:1, o + 6:o + 7],
                             AF.Exp, scale=0.5)
                g.activation(scal[0:1, o + 10:o + 11], scal[0:1, o + 7:o + 8],
                             AF.Exp, scale=0.5)
            with pr.g("dve", reads=[b_scal], writes=[b_scal]) as g:
                g.tensor_tensor(scal[0:1, o + 11:o + 12], scal[0:1, o + 8:o + 9],
                                scal[0:1, o + 9:o + 10], ALU.add)
            with pr.g("dve", reads=[b_scal], writes=[b_scal]) as g:
                g.tensor_tensor(scal[0:1, o + 12:o + 13], scal[0:1, o + 11:o + 12],
                                scal[0:1, o + 10:o + 11], ALU.add)
            with pr.g("dve", reads=[b_scal], writes=[b_scal]) as g:
                g.tensor_scalar(scal[0:1, o + 13:o + 14], scal[0:1, o + 12:o + 13],
                                -0.125, None, ALU.mult)
            with pr.g("dve", reads=[b_scal], writes=[b_sclb]) as g:
                g.tensor_copy(scalb[0:1, hh:hh + 1], scal[0:1, o + 13:o + 14])
            with pr.g("pe", reads=[b_sclb, b_1], writes=[b_kd[0]]) as g:
                g.matmul(ps_kd[0][:, 0:1], ones[0:1, 0:128], scalb[0:1, hh:hh + 1],
                         start=True, stop=True)
            with pr.g("dve", reads=[b_kd[0]], writes=[b_bm]) as g:
                g.tensor_copy(biasm[:, hh:hh + 1], ps_kd[0][:, 0:1])

    def sat_rows(p):
        for hh in range(2):
            hb = 64 * hh
            jobs = [(bloc, ecol, T, bT)
                    for (bloc, ecol, T, bT) in [
                        (0, PAD, KT, b_KT), (1, PAD + JW - 1, KT, b_KT),
                        (2, PAD, QT, b_QT), (3, PAD + JW - 1, QT, b_QT)]
                    for cc in [None]]
            k = 0
            for (bloc, ecol, T, bT) in [(0, PAD, KT, b_KT), (1, PAD + JW - 1, KT, b_KT),
                                        (2, PAD, QT, b_QT), (3, PAD + JW - 1, QT, b_QT)]:
                for cc in range(0, S, 512):
                    pst, bt = (ps_kd[k % 2], b_kd[k % 2])
                    with pr.g("pe", reads=[b_ee, bT], writes=[bt]) as g:
                        g.matmul(pst[0:1, 0:512], eext[hb:hb + 64, ecol:ecol + 1],
                                 T[hb:hb + 64, cc:cc + 512], start=True, stop=True)
                    with pr.g("dve", reads=[bt], writes=[b_sat]) as g:
                        g.tensor_copy(
                            satr[hb:hb + 1, bloc * S + cc:bloc * S + cc + 512],
                            pst[0:1, 0:512])
                    k += 1

    def qd_prologue(p, lq):
        l_base = lq * LQ
        for i in range(NLTQ):
            l0 = l_base + 128 * i
            rts = [r0 for r0 in range(0, S, 128) if c.kind(l0, r0) == "band"]
            r_lo, r_hi = rts[0], rts[-1] + 128
            w_lo = r_lo - (l0 + 127) + MAXP - 1 + PAD
            w_hi = (r_hi - 1) - l0 + MAXP - 1 + PAD + 1
            assert 0 <= w_lo and w_hi <= EXTW, (w_lo, w_hi)
            c0 = w_lo
            for ci, cc in enumerate(range(c0, w_hi, 512)):
                cw = min(512, EXTW - cc)
                pst = ps_kd if ci % 2 == 0 else ps_qd
                bst = b_kd if ci % 2 == 0 else b_qdp
                with pr.g("pe", reads=[b_QT, b_er], writes=[bst[0], bst[1]]) as g:
                    for hh in range(2):
                        hb = 64 * hh
                        g.matmul(pst[hh][:, 0:cw], QT[hb:hb + 64, l0:l0 + 128],
                                 erev[hb:hb + 64, cc:cc + cw], start=True, stop=True)
                for hh in range(2):
                    ev = "dve" if hh == 0 else "act"
                    dst = qdst[:, EXTW * hh + cc:EXTW * hh + cc + cw]
                    with pr.g(ev, reads=[bst[hh]], writes=[b_qd[hh][ci % 6]]) as g:
                        if ev == "dve":
                            g.tensor_copy(dst, pst[hh][:, 0:cw])
                        else:
                            g.copy(dst, pst[hh][:, 0:cw])
            for hh in range(2):
                off = EXTW * hh + (r_lo - l0 + MAXP - 1 + PAD)
                src = AP(qdst.ap().tensor, off, [(2 * EXTW - 1, 128), (1, r_hi - r_lo)])
                pr.dma("qsp" if hh == 0 else "qact",
                       out=QPE[hh][:, S * i + r_lo:S * i + r_hi], in_=src,
                       reads=list(b_qd[hh]), writes=[b_QPE[hh][i]])

    def kd_strip(p, lq, r0, slot):
        l_base = lq * LQ
        lts = [l0 for l0 in range(l_base, l_base + LQ, 128) if c.kind(l0, r0) == "band"]
        if not lts:
            return None
        l_lo, l_hi = lts[0], lts[-1] + 128
        u_lo = l_lo - (r0 + 127) + MAXP - 1 + PAD
        u_hi = (l_hi - 1) - r0 + MAXP - 1 + PAD + 1
        assert 0 <= u_lo and u_hi <= EXTW, (u_lo, u_hi)
        c0 = u_lo
        assert u_hi - c0 <= KWIN
        ksl = slot % 2
        for ci, cc in enumerate(range(c0, u_hi, 512)):
            cw = min(512, EXTW - cc)
            pst = ps_kd if ci % 2 == 0 else ps_qd
            bst = b_kd if ci % 2 == 0 else b_qdp
            with pr.g("pe", reads=[b_KT, b_ee], writes=[bst[0], bst[1]]) as g:
                for hh in range(2):
                    hb = 64 * hh
                    g.matmul(pst[hh][:, 0:cw], KT[hb:hb + 64, r0:r0 + 128],
                             eext[hb:hb + 64, cc:cc + cw], start=True, stop=True)
            for hh in range(2):
                ev = "dve" if hh == 0 else "act"
                dst = kdst[:, KWIN * (2 * ksl + hh) + (cc - c0):
                           KWIN * (2 * ksl + hh) + (cc - c0) + cw]
                with pr.g(ev, reads=[bst[hh]], writes=[b_kd_sb[ksl][hh][ci]]) as g:
                    if ev == "dve":
                        g.tensor_copy(dst, pst[hh][:, 0:cw])
                    else:
                        g.copy(dst, pst[hh][:, 0:cw])
        for hh in range(2):
            ksl = slot % 2
            off = KWIN * (2 * ksl + hh) + (l_lo - r0 + MAXP - 1 + PAD - c0)
            src = AP(kdst.ap().tensor, off, [(4 * KWIN - 1, 128), (1, l_hi - l_lo)])
            pr.dma("qsp" if hh == 0 else "qact",
                   out=KPE[:, (3 * hh + slot) * LQ:(3 * hh + slot) * LQ + (l_hi - l_lo)],
                   in_=src, reads=list(b_kd_sb[ksl][hh]), writes=[b_KPE[hh][slot]])
        return (l_lo, l_hi)

    def score_pair(p, lq, r0, kwin, kslot, slot):
        l_base = lq * LQ
        opsh = [[], []]
        reads = [b_KT, b_QT, b_id]
        for hh in range(2):
            hb = 64 * hh

            def qk(g, st, sp, hh=hh, hb=hb):
                g.matmul(ps_sc[hh][slot][:, 0:LQ], KT[hb:hb + 64, r0:r0 + 128],
                         QT[hb:hb + 64, l_base:l_base + LQ], start=st, stop=sp)
            opsh[hh].append(qk)
        for (side, kbloc, qbloc) in [("hi", 0, 2), ("lo", 1, 3)]:
            sls = [l0 for l0 in range(l_base, l_base + LQ, 128) if c.kind(l0, r0) == side]
            if not sls:
                continue
            s_lo, s_hi = sls[0] - l_base, sls[-1] + 128 - l_base
            reads += [b_sat, b_1]
            for hh in range(2):
                hb = 64 * hh

                def sat1(g, st, sp, s_lo=s_lo, s_hi=s_hi, kbloc=kbloc, hh=hh, hb=hb):
                    g.matmul(ps_sc[hh][slot][:, s_lo:s_hi],
                             satr[hb:hb + 1, kbloc * S + r0:kbloc * S + r0 + 128],
                             ones[hb:hb + 1, 0:s_hi - s_lo], start=st, stop=sp)

                def sat2(g, st, sp, s_lo=s_lo, s_hi=s_hi, qbloc=qbloc, hh=hh, hb=hb):
                    g.matmul(ps_sc[hh][slot][:, s_lo:s_hi], ones[hb:hb + 1, 0:128],
                             satr[hb:hb + 1, qbloc * S + l_base + s_lo:
                                  qbloc * S + l_base + s_hi], start=st, stop=sp)
                opsh[hh] += [sat1, sat2]
        if kwin is not None:
            l_lo, l_hi = kwin
            for l0 in range(l_lo, l_hi, 128):
                i = (l0 - l_base) // 128
                for hh in range(2):
                    def qop(g, st, sp, l0=l0, i=i, hh=hh):
                        g.matmul(ps_sc[hh][slot][:, l0 - l_base:l0 - l_base + 128],
                                 QPE[hh][:, S * i + r0:S * i + r0 + 128], ident[:],
                                 start=st, stop=sp)
                    opsh[hh].append(qop)
                reads += [b_QPE[0][i], b_QPE[1][i]]
        dedup, seen = [], set()
        for b in reads:
            if id(b) not in seen:
                seen.add(id(b))
                dedup.append(b)
        nops = len(opsh[0])
        with pr.g("pe", reads=dedup, writes=[b_sc[0][slot], b_sc[1][slot]]) as g:
            for k in range(nops):
                for hh in range(2):
                    opsh[hh][k](g, k == 0, k == nops - 1 and kwin is None)
        if kwin is not None:
            with pr.g("pe", reads=[b_KPE[0][kslot], b_KPE[1][kslot], b_id],
                      writes=[b_sc[0][slot], b_sc[1][slot]]) as g:
                for hh in range(2):
                    g.matmul(ps_sc[hh][slot][:, l_lo - l_base:l_hi - l_base], ident[:],
                             KPE[:, (3 * hh + kslot) * LQ:(3 * hh + kslot) * LQ + (l_hi - l_lo)],
                             start=False, stop=True)

    def exp_tile(p, lq, r0, hh, slot):
        with pr.g("act", reads=[b_sc[hh][slot], b_bm], writes=[b_PT[hh][slot]]) as g:
            g.activation(PT[:, (2 * hh + slot) * LQ:(2 * hh + slot) * LQ + LQ],
                         ps_sc[hh][slot][:, 0:LQ], AF.Exp,
                         bias=biasm[:, hh:hh + 1], scale=0.125)

    def pv_pair(p, lq, r0, slot, first, last):
        t = r0 // 128
        with pr.g("pe", reads=[b_V, b_PT[0][slot], b_PT[1][slot]],
                  writes=[b_ctx[0], b_ctx[1]]) as g:
            for hh in range(2):
                vcol = 129 * t + (0 if hh == 0 else 64)
                g.matmul(ps_ctx[hh][:, 0:LQ], Vsb[:, vcol:vcol + 65],
                         PT[:, (2 * hh + slot) * LQ:(2 * hh + slot) * LQ + LQ],
                         start=first, stop=last)

    def ctx_out(p, lq):
        l_base = lq * LQ
        for hh in range(2):
            h = 2 * p + hh
            with pr.g("dve", reads=[b_ctx[hh]], writes=[b_ctxsb[hh]]) as g:
                g.tensor_copy(ctxsb[:, hh * LQ:hh * LQ + LQ], ps_ctx[hh][:, 0:LQ])
            for i in range(NLTQ):
                l0 = l_base + 128 * i
                tsl = 4 * hh + i % 4
                with pr.g("pe", reads=[b_ctxsb[hh], b_idf], writes=[b_kd[hh]]) as g:
                    g.transpose(ps_kd[hh][:, 0:65],
                                ctxsb[0:65, hh * LQ + 128 * i:hh * LQ + 128 * i + 128],
                                idf32[0:65, 0:65])
                with pr.g("dve", reads=[b_kd[hh]], writes=[b_trp[tsl]]) as g:
                    g.tensor_copy(trp[:, 65 * tsl:65 * tsl + 65], ps_kd[hh][:, 0:65])
                pr.dma("qsp", out=ctx_dram[l0:l0 + 128, 65 * h:65 * h + 65],
                       in_=trp[:, 65 * tsl:65 * tsl + 65],
                       reads=[b_trp[tsl], b_ctxd[l0 // 128]])

    for p in range(c.NPAIR if stages >= 90 else (1 if stages >= 2 else 0)):
        projections(p)
        if stages >= 3:
            bounds(p)
        if stages >= 4:
            sat_rows(p)
        if p == 0:
            dump("QT0", QT[:], (128, S), BF16, [b_QT])
            dump("KT0", KT[:], (128, S), BF16, [b_KT])
            dump("Vsb", Vsb[:], (128, NLT * 129), BF16, [b_V])
            dump("satr", satr[:], (128, 4 * S), BF16, [b_sat])
            dump("biasm", biasm[:], (128, 2), F32, [b_bm])
        nlq_eff = NLQ if stages >= 80 else (1 if stages >= 5 else 0)
        for lq in range(nlq_eff):
            qd_prologue(p, lq)
            if p == 0 and lq == 0:
                dump("QPE0", QPE[0][:], (128, NLTQ * S), BF16, b_QPE[0])
            rmax = S if stages >= 80 else (128 if stages >= 6 else 0)
            l_mid = lq * LQ + LQ // 2
            r0s = sorted(range(0, rmax, 128), key=lambda r: -abs(r + 64 - l_mid))
            kwins = {}
            if rmax:
                kwins[r0s[0]] = kd_strip(p, lq, r0s[0], 0)
            if len(r0s) > 1:
                kwins[r0s[1]] = kd_strip(p, lq, r0s[1], 1)
            for ri, r0 in enumerate(r0s):
                slot = ri % 2
                if p == 0 and lq == 0 and r0 == 0:
                    dump("KPE0", KPE[:], (128, 6 * LQ), BF16,
                         [bb for pair in b_KPE for bb in pair])
                if stages >= 7:
                    score_pair(p, lq, r0, kwins[r0], ri % 3, slot)
                    if stages >= 71:
                        for hh in range(2):
                            exp_tile(p, lq, r0, hh, slot)
                if ri + 2 < len(r0s):
                    kwins[r0s[ri + 2]] = kd_strip(p, lq, r0s[ri + 2], (ri + 2) % 3)
                if stages >= 72:
                    pv_pair(p, lq, r0, slot, ri == 0, ri == len(r0s) - 1)
                if p == 0 and lq == 0 and r0 == 0:
                    dump("PT0", PT[:], (128, 4 * LQ), BF16,
                         [bb for pair in b_PT for bb in pair])
            if stages >= 80:
                ctx_out(p, lq)

    es2.close()

    resid = sb("resid", [128, 2 * E], F32, es=es3)
    b_res2 = [Buf(), Buf()]
    sqf = sb("sqf", [128, 2 * E], F32, es=es3)
    b_sqf2 = [Buf(), Buf()]
    ctxt = sb("ctxt", [128, 2 * c.H * 65], F32, es=es3)
    b_ct2 = [Buf(), Buf()]
    xt_sb = sb("xt_sb", [128, 4 * E], F32, es=es3)
    b_xts2 = [Buf() for _ in range(4)]
    ln1bf = sb("ln1bf", [128, 2 * E], es=es3)
    b_l1b2 = [Buf(), Buf()]
    ln1T = sb("ln1T", [128, NEC * S], es=es3)
    b_l1T = [Buf() for _ in range(NLT)]
    ps_tb = psum("ps_tb", [128, 2048], BF16, es=es3)
    b_tb = [Buf(), Buf()]

    def ln_stats(b_src, src_sb, sl=0):
        b_st = b_st2[sl]
        so = 16 * sl
        nh = (E + 511) // 512
        with pr.g("dve", reads=[b_src], writes=[b_st]) as g:
            for hhh in range(nh):
                w = min(512, E - 512 * hhh)
                g.bn_stats(stat[:, so + 6 * hhh:so + 6 * hhh + 6],
                           src_sb[:, 512 * hhh:512 * hhh + w])
        with pr.g("dve", reads=[b_st], writes=[b_st]) as g:
            g.bn_aggr(stat[:, so + 12:so + 14], stat[:, so:so + 6 * nh])
        with pr.g("dve", reads=[b_st], writes=[b_st]) as g:
            g.tensor_scalar(stat[:, so + 14:so + 15], stat[:, so + 13:so + 14], 1.0,
                            1e-5, ALU.mult, ALU.add)
        with pr.g("act", reads=[b_st], writes=[b_st]) as g:
            g.activation(stat[:, so + 15:so + 16], stat[:, so + 14:so + 15], AF.Ln)
        with pr.g("act", reads=[b_st], writes=[b_st]) as g:
            g.activation(stat[:, so + 10:so + 11], stat[:, so + 15:so + 16],
                         AF.Exp, scale=-0.5)
        with pr.g("dve", reads=[b_src, b_st], writes=[b_src]) as g:
            g.tensor_scalar(src_sb, src_sb, stat[:, so + 12:so + 13],
                            stat[:, so + 10:so + 11], ALU.subtract, ALU.mult)

    HC = c.H * 65

    def ln1_tile(t):
        l0 = 128 * t
        sl = t % 2
        b_xts, b_ct = b_xts2[sl], b_ct2[sl]
        b_res = b_res2[sl]
        rs_v = resid[:, sl * E:sl * E + E]
        pr.dma("qsp", out=xt_sb[:, sl * E:sl * E + E], in_=x_tok[l0:l0 + 128, 0:E],
               writes=[b_xts])
        pr.dma("qsp", out=ctxt[:, sl * HC:sl * HC + HC],
               in_=ctx_dram[l0:l0 + 128, 0:HC], writes=[b_ct, b_ctxd[t]])
        with pr.g("dve", reads=[b_ct], writes=[b_ct]) as g:
            for h in range(c.H):
                zc = sl * HC + 65 * h + (64 if h % 2 == 0 else 0)
                g.reciprocal(ctxt[:, zc:zc + 1], ctxt[:, zc:zc + 1])
        with pr.g("dve", reads=[b_ct], writes=[b_res]) as g:
            for h in range(c.H):
                zc = sl * HC + 65 * h + (64 if h % 2 == 0 else 0)
                cc = sl * HC + 65 * h + (0 if h % 2 == 0 else 1)
                g.tensor_scalar(rs_v[:, 64 * h:64 * h + 64], ctxt[:, cc:cc + 64],
                                ctxt[:, zc:zc + 1], None, ALU.mult)
        with pr.g("dve", reads=[b_xts, b_res], writes=[b_res]) as g:
            g.tensor_tensor(rs_v, rs_v, xt_sb[:, sl * E:sl * E + E], ALU.add)
        ln_stats(b_res, rs_v, sl)
        pr.dma("qsp", out=ln1_dram[l0:l0 + 128, 0:E], in_=rs_v, reads=[b_res],
               writes=[b_ln1d[t]])
        with pr.g("dve", reads=[b_res], writes=[b_l1b2[sl]]) as g:
            g.tensor_copy(ln1bf[:, sl * E:sl * E + E], rs_v)
        for ec in range(NEC):
            esl = ec % 2
            with pr.g("pe", reads=[b_l1b2[sl], b_id], writes=[b_tb[esl]]) as g:
                g.transpose(ps_tb[:, 1024 * esl:1024 * esl + 128],
                            ln1bf[:, sl * E + 128 * ec:sl * E + 128 * ec + 128], ident[:])
            with pr.g("dve", reads=[b_tb[esl]], writes=[b_l1T[t]]) as g:
                g.tensor_copy(ln1T[:, ec * S + l0:ec * S + l0 + 128],
                              ps_tb[:, 1024 * esl:1024 * esl + 128])
        if t == 0:
            dump("ln1_0", resid[:, 0:E], (128, E), F32, [b_res2[0]])

    load_cast(wbuf[0], b_w[0], wr_d, E, E)
    load_cast(wbuf[1], b_w[1], wkf_d, E, E)
    load_cast(wbuf[2], b_w[2], wvf_d, E, E)
    mur = sb("mur_sb", [128, NEC], F32, es=es3); b_mur = Buf()
    muk = sb("muk_sb", [128, NEC], F32, es=es3); b_muk = Buf()
    murc = sb("murc", [128, 2 * NEC], F32, es=es3); b_murc = Buf()
    for (t_, b_, src) in [(mur, b_mur, mur_d), (muk, b_muk, muk_d)]:
        pr.dma("qsp", out=ld[0:128, 0:NEC], in_=src, writes=[b_ld[0]])
        with pr.g("dve", reads=[b_ld[0]], writes=[b_]) as g:
            g.tensor_copy(t_[:], ld[0:128, 0:NEC])
    with pr.g("dve", reads=[b_mur, b_muk], writes=[b_murc]) as g:
        g.tensor_scalar(murc[:, 0:NEC], mur[:], -1.0, 1.0, ALU.mult, ALU.add)
        g.tensor_scalar(murc[:, NEC:2 * NEC], muk[:], -1.0, 1.0, ALU.mult, ALU.add)

    LB = LQ
    xrT = sb("xrT", [128, NEC * LB], es=es3); b_xr = Buf()
    xkT = sb("xkT", [128, NEC * LB], es=es3); b_xk = Buf()
    rsb = sb("rsb", [128, NEC * LB], F32, es=es3); b_r = Buf()
    kact = sb("kact", [128, NEC * LB], es=es3); b_ka = Buf()
    ffnT = sb("ffnT", [128, NEC * LB], F32, es=es3); b_ff = Buf()
    otile = sb("otile", [128, (512 // 128) * E], F32, es=es3); b_ot = Buf()

    def ffn_block(bl):
        l0 = bl * LB
        t_lo = max(0, bl * LB // 128 - 1)
        t_hi = min(NLT, (bl * LB + LB) // 128)
        l1deps = [b_l1T[tt] for tt in range(t_lo, t_hi)]
        with pr.g("dve", reads=l1deps + [b_murc], writes=[b_xr, b_xk]) as g:
            for ec in range(NEC):
                if l0 == 0:
                    g.memset(xrT[:, ec * LB:ec * LB + 1], 0.0)
                    g.memset(xkT[:, ec * LB:ec * LB + 1], 0.0)
                    d0, sh = 1, 0
                else:
                    d0, sh = 0, -1
                g.tensor_scalar(xrT[:, ec * LB + d0:ec * LB + LB],
                                ln1T[:, ec * S + l0 + sh:ec * S + l0 + LB - 1],
                                murc[:, ec:ec + 1], None, ALU.mult)
                g.tensor_scalar(xkT[:, ec * LB + d0:ec * LB + LB],
                                ln1T[:, ec * S + l0 + sh:ec * S + l0 + LB - 1],
                                murc[:, NEC + ec:NEC + ec + 1], None, ALU.mult)
        with pr.g("dve", reads=l1deps + [b_mur, b_muk, b_xr, b_xk],
                  writes=[b_xr, b_xk]) as g:
            for ec in range(NEC):
                g.scalar_tensor_tensor(xrT[:, ec * LB:ec * LB + LB],
                                       ln1T[:, ec * S + l0:ec * S + l0 + LB],
                                       mur[:, ec:ec + 1], xrT[:, ec * LB:ec * LB + LB],
                                       ALU.mult, ALU.add)
                g.scalar_tensor_tensor(xkT[:, ec * LB:ec * LB + LB],
                                       ln1T[:, ec * S + l0:ec * S + l0 + LB],
                                       muk[:, ec:ec + 1], xkT[:, ec * LB:ec * LB + LB],
                                       ALU.mult, ALU.add)
        for dc in range(NEC):
            pr1, br1 = (ps_kd[0], b_kd[0]) if dc % 2 == 0 else (ps_qd[0], b_qdp[0])
            pr2, br2 = (ps_kd[1], b_kd[1]) if dc % 2 == 0 else (ps_qd[1], b_qdp[1])
            with pr.g("pe", reads=[b_w[0], b_xr], writes=[br1]) as g:
                for ec in range(NEC):
                    g.matmul(pr1[:, 0:LB],
                             wbuf[0][:, ec * E + 128 * dc:ec * E + 128 * dc + 128],
                             xrT[:, ec * LB:ec * LB + LB],
                             start=(ec == 0), stop=(ec == NEC - 1))
            with pr.g("act", reads=[br1], writes=[b_r]) as g:
                g.activation(rsb[:, dc * LB:dc * LB + LB], pr1[:, 0:LB], AF.Sigmoid)
            with pr.g("pe", reads=[b_w[1], b_xk], writes=[br2]) as g:
                for ec in range(NEC):
                    g.matmul(pr2[:, 0:LB],
                             wbuf[1][:, ec * E + 128 * dc:ec * E + 128 * dc + 128],
                             xkT[:, ec * LB:ec * LB + LB],
                             start=(ec == 0), stop=(ec == NEC - 1))
            with pr.g("act", reads=[br2], writes=[b_ka]) as g:
                g.activation(kact[:, dc * LB:dc * LB + LB], pr2[:, 0:LB], AF.Relu)
            with pr.g("act", reads=[b_ka], writes=[b_ka]) as g:
                g.activation(kact[:, dc * LB:dc * LB + LB],
                             kact[:, dc * LB:dc * LB + LB], AF.Square)
        NSUB = LB // 128
        for i in range(NSUB):
            lt = l0 + 128 * i
            pr.dma("qsp", out=xt_sb[:, i * E:i * E + E],
                   in_=ln1_dram[lt:lt + 128, 0:E], reads=[b_ln1d[lt // 128]],
                   writes=[b_xts2[i]])
        b_ffc = [Buf() for _ in range(NEC)]
        for dc in range(NEC):
            psv, bsv = (ps_sc[0][1], b_sc[0][1]) if dc % 2 == 0 else (ps_sc[1][1], b_sc[1][1])
            with pr.g("pe", reads=[b_w[2], b_ka], writes=[bsv]) as g:
                for ec in range(NEC):
                    g.matmul(psv[:, 0:LB],
                             wbuf[2][:, ec * E + 128 * dc:ec * E + 128 * dc + 128],
                             kact[:, ec * LB:ec * LB + LB],
                             start=(ec == 0), stop=(ec == NEC - 1))
            with pr.g("dve", reads=[bsv, b_r], writes=[b_ffc[dc]]) as g:
                g.tensor_tensor(ffnT[:, dc * LB:dc * LB + LB], psv[:, 0:LB],
                                rsb[:, dc * LB:dc * LB + LB], ALU.mult)
            for i in range(NSUB):
                sl = (dc + i) % 2
                with pr.g("pe", reads=[b_ffc[dc], b_idf], writes=[b_kd[sl]]) as g:
                    g.transpose(ps_kd[sl][:, 0:128],
                                ffnT[0:128, dc * LB + 128 * i:dc * LB + 128 * i + 128],
                                idf32[:])
                with pr.g("dve", reads=[b_kd[sl], b_xts2[i]], writes=[b_ot]) as g:
                    g.tensor_tensor(otile[:, i * E + 128 * dc:i * E + 128 * dc + 128],
                                    ps_kd[sl][:, 0:128],
                                    xt_sb[:, i * E + 128 * dc:i * E + 128 * dc + 128],
                                    ALU.add)
        for i in range(NSUB):
            lt = l0 + 128 * i
            ln_stats(b_ot, otile[:, i * E:i * E + E], i % 2)
            pr.dma("qsp", out=out_t[lt:lt + 128, 0:E], in_=otile[:, i * E:i * E + E],
                   reads=[b_ot])
        if bl == 0:
            dump("xrT0", xrT[:], (128, NEC * LB), BF16, [b_xr])
            dump("ffnT0", ffnT[:], (128, NEC * LB), F32, b_ffc)

    TPB = LB // 128
    if stages >= 100:
        for t in range(NLT):
            ln1_tile(t)
            if stages >= 110 and (t + 1) % TPB == 0:
                ffn_block((t + 1) // TPB - 1)

    for i in range(4):
        if pr.cnt[f"qsp{i}"]:
            nc.sync.wait_ge(pr.sem[f"qsp{i}"], pr.cnt[f"qsp{i}"])
        if pr.cnt[f"qact{i}"]:
            nc.sync.wait_ge(pr.sem[f"qact{i}"], pr.cnt[f"qact{i}"])
    es3.close()
    pr.es.close()
    return nc, dbg_used


def host_inputs(cfg, x_b, wq, bq, wk, bk, wv, bv, dist, W_r, W_k, W_v, mu_r, mu_k):
    c = cfg
    ar = lambda a: np.ascontiguousarray(a, np.float32)
    idx = np.arange(c.EXTW) - c.PAD
    e_ext = dist[np.clip(idx, 0, c.JW - 1)]
    e_rev = dist[np.clip(c.JW - 1 - idx, 0, c.JW - 1)]
    em = {
        "x_tok": ar(x_b), "x_fm": ar(x_b.T),
        "wqT": ar(wq.T), "wkT": ar(wk.T), "wvT": ar(wv.T),
        "bqv": ar(bq.reshape(1, -1)), "bkv": ar(bk.reshape(1, -1)),
        "bvv": ar(bv.reshape(1, -1)),
        "e_ext": ar(np.concatenate([e_ext.T, e_ext.T], axis=0)),
        "e_rev": ar(np.concatenate([e_rev.T, e_rev.T], axis=0)),
        "wrT": ar(W_r.T), "wkfT": ar(W_k.T), "wvfT": ar(W_v.T),
        "mur": ar(mu_r.reshape(c.NEC, 128).T),
        "muk": ar(mu_k.reshape(c.NEC, 128).T),
        "ident": np.eye(128, dtype=BF),
        "identf": np.eye(128, dtype=np.float32),
        "ones_bf": np.ones((128, 512), dtype=BF),
    }
    return em


def kernel(L0_attn_hidden_states, L0_attn_wq, L0_attn_bq, L0_attn_wk, L0_attn_bk,
           L0_attn_wv, L0_attn_bv, L0_attn_dist_emb, L0_ffn_W_r, L0_ffn_W_k,
           L0_ffn_W_v, L0_ffn_mu_r, L0_ffn_mu_k):
    cfg = Cfg(S=2048, E=1024, MAXP=1024)
    f = lambda a: np.asarray(a, np.float32)
    x = np.ascontiguousarray(f(L0_attn_hidden_states))
    nc, _ = build_program(cfg)
    in_maps = [host_inputs(cfg, x[b], f(L0_attn_wq), f(L0_attn_bq), f(L0_attn_wk),
                           f(L0_attn_bk), f(L0_attn_wv), f(L0_attn_bv),
                           f(L0_attn_dist_emb), f(L0_ffn_W_r), f(L0_ffn_W_k),
                           f(L0_ffn_W_v), f(L0_ffn_mu_r), f(L0_ffn_mu_k))
               for b in range(8)]
    res = run_bass_kernel_spmd(nc, in_maps, core_ids=list(range(8)))
    return np.stack([np.asarray(res.results[b]["out"], np.float32) for b in range(8)])


# version 27
# speedup vs baseline: 1.3081x; 1.0034x over previous
import os
import numpy as np
import ml_dtypes
import concourse.bass as bass
import concourse.mybir as mybir
from concourse.ap import AP
from concourse.bass_utils import run_bass_kernel_spmd
import concourse.bass_utils as _bu
from contextlib import ExitStack

_orig_run_command = _bu.run_command


def _run_command_ldwopt(cmd, *a, **kw):
    if isinstance(cmd, (list, tuple)):
        cmd = ["--enable-ldw-opt=true" if c == "--enable-ldw-opt=false" else c
               for c in cmd]
    return _orig_run_command(cmd, *a, **kw)


_bu.run_command = _run_command_ldwopt

F32 = mybir.dt.float32
BF16 = mybir.dt.bfloat16
AF = mybir.ActivationFunctionType
ALU = mybir.AluOpType
BF = ml_dtypes.bfloat16
AX = mybir.AxisListType.X


class Cfg:
    def __init__(self, S=2048, E=1024, MAXP=1024):
        self.S, self.E, self.MAXP = S, E, MAXP
        self.D = 64
        self.H = E // 64
        self.NPAIR = self.H // 2
        self.JW = 2 * MAXP - 1
        self.PAD = 255
        self.EXTW = self.JW + 2 * self.PAD
        self.NLT = S // 128
        self.LQ = min(512, S)
        self.NLQ = S // self.LQ
        self.NLTQ = self.LQ // 128
        self.NEC = E // 128
        self.SAT = MAXP + 128

    def kind(self, l0, r0):
        if l0 - r0 >= self.SAT:
            return "lo"
        if r0 - l0 >= self.SAT:
            return "hi"
        return "band"


class Buf:
    __slots__ = ("writer", "readers")

    def __init__(self):
        self.writer = None
        self.readers = []


def build_program(cfg, dbg=(), stages=999):
    c = cfg
    S, E, NEC = c.S, c.E, c.NEC
    MAXP, JW, PAD, EXTW = c.MAXP, c.JW, c.PAD, c.EXTW
    LQ, NLQ, NLTQ, NLT = c.LQ, c.NLQ, c.NLTQ, c.NLT
    KWIN = ((LQ + 255 + 511) // 512 + 1) * 512
    nc = bass.Bass("TRN2", target_bir_lowering=False, debug=False)

    def din(name, shape, dt=F32):
        return nc.dram_tensor(name, shape, dt, kind="ExternalInput").ap()

    x_tok = din("x_tok", [S, E])
    x_fm = din("x_fm", [E, S])
    wqT_d = din("wqT", [E, E]); wkT_d = din("wkT", [E, E]); wvT_d = din("wvT", [E, E])
    bq_d = din("bqv", [1, E]); bk_d = din("bkv", [1, E]); bv_d = din("bvv", [1, E])
    eext_d = din("e_ext", [128, EXTW])
    erev_d = din("e_rev", [128, EXTW])
    wr_d = din("wrT", [E, E]); wkf_d = din("wkfT", [E, E]); wvf_d = din("wvfT", [E, E])
    mur_d = din("mur", [128, NEC]); muk_d = din("muk", [128, NEC])
    ident_d = din("ident", [128, 128], BF16)
    identf_d = din("identf", [128, 128], F32)
    ones_d = din("ones_bf", [128, 512], BF16)
    out_t = nc.dram_tensor("out", [S, E], F32, kind="ExternalOutput").ap()
    ctx_dram = nc.dram_tensor("ctx_scr", [S, c.H * 65], F32, kind="Internal").ap()
    ln1_dram = nc.dram_tensor("ln1_scr", [S, E], F32, kind="Internal").ap()
    dbg_used = []

    NS = 8

    class Prog:
        def __init__(self):
            self.eng = {"pe": nc.tensor, "dve": nc.vector, "act": nc.scalar,
                        "sp": nc.sync}
            self.es = ExitStack()
            names = (["pe", "dve", "act", "sp"]
                     + [f"qsp{i}" for i in range(NS)] + [f"qact{i}" for i in range(NS)])
            self.sem = {k: self.es.enter_context(nc.semaphore("sem_" + k))
                        for k in names}
            self.cnt = {k: 0 for k in self.sem}
            self.qtot = {"qsp": 0, "qact": 0}
            self.waited = {}
            self.issuer = {"qsp": "sp", "qact": "act"}

        def _wait(self, eng, dep, val):
            assert val <= self.cnt[dep], f"future wait {eng}<-{dep}@{val}"
            if self.waited.get((eng, dep), -1) >= val:
                return
            self.eng[eng].wait_ge(self.sem[dep], val)
            self.waited[(eng, dep)] = val

        def _pre(self, eng, reads, writes):
            deps = {}

            def add(e, cc):
                if deps.get(e, 0) < cc:
                    deps[e] = cc
            for b in reads:
                if b.writer:
                    add(*b.writer)
            for b in writes:
                if b.writer:
                    add(*b.writer)
                for rc in b.readers:
                    add(*rc)
            for e in sorted(deps):
                if e == eng == "pe":
                    continue
                self._wait(eng, e, deps[e])

        def g(self, eng, reads=(), writes=()):
            pr = self

            class G:
                def __init__(gg):
                    gg.last = None

                def __getattr__(gg, item):
                    fn = getattr(pr.eng[eng], item)

                    def wrap(*a, **k):
                        inst = fn(*a, **k)
                        gg.last = inst
                        return inst
                    return wrap

            class Ctx:
                def __enter__(ctx):
                    pr._pre(eng, reads, writes)
                    ctx.grp = G()
                    return ctx.grp

                def __exit__(ctx, et, ev, tb):
                    if et is not None:
                        return False
                    assert ctx.grp.last is not None
                    ctx.grp.last.then_inc(pr.sem[eng], 1)
                    pr.cnt[eng] += 1
                    mark = (eng, pr.cnt[eng])
                    for b in writes:
                        b.writer = mark
                        b.readers = []
                    for b in reads:
                        b.readers.append(mark)
                    return False
            return Ctx()

        def dma(self, q, out, in_, reads=(), writes=()):
            eng = self.issuer[q]
            s = f"{q}{self.qtot[q] % NS}"
            self.qtot[q] += 1
            self._pre(eng, reads, writes)
            if self.cnt[s] > 0:
                self._wait(eng, s, self.cnt[s])
            self.eng[eng].dma_start(out=out, in_=in_).then_inc(self.sem[s], 16)
            self.cnt[s] += 16
            mark = (s, self.cnt[s])
            for b in writes:
                b.writer = mark
                b.readers = []
            for b in reads:
                b.readers.append(mark)

    pr = Prog()
    b_ctxd = [Buf() for _ in range(NLT)]
    b_ln1d = [Buf() for _ in range(NLT)]
    es2 = ExitStack()
    es3 = ExitStack()

    def sb(n, shape, dt=BF16, es=None):
        return (es or pr.es).enter_context(nc.sbuf_tensor(n, shape, dt))

    def psum(n, shape, dt=F32, es=None):
        return (es or pr.es).enter_context(nc.psum_tensor(n, shape, dt))

    def dump(name, src, shape, dt, reads):
        if name not in dbg:
            return
        o = nc.dram_tensor("dbg_" + name, list(shape), dt, kind="ExternalOutput").ap()
        dbg_used.append(name)
        pr.dma("qsp", out=o, in_=src, reads=list(reads))

    LDW_ = (max(S, E) + 511) // 512 * 512 // 2
    ld = sb("ld", [128, 2 * LDW_], F32)
    b_ld = [Buf(), Buf()]
    wbuf = [sb(f"wbuf{i}", [128, NEC * E]) for i in range(3)]
    b_w = [Buf() for _ in range(3)]
    ident = sb("identity", [128, 128]); b_id = Buf()
    idf32 = sb("idf32b", [128, 128], F32); b_idf = Buf()
    ones = sb("ones", [128, 512]); b_1 = Buf()
    stat = sb("stat", [128, 32], F32)
    b_st2 = [Buf(), Buf()]

    xT = sb("xT", [128, NEC * S], es=es2); b_xT = Buf()
    eext = sb("eext", [128, EXTW], es=es2); b_ee = Buf()
    erev = sb("erev", [128, EXTW], es=es2); b_er = Buf()
    biasq = sb("biasq", [1, E], es=es2); b_bq = Buf()
    biask = sb("biask", [1, E], es=es2); b_bk = Buf()
    biasv = sb("biasv", [1, E], es=es2); b_bv = Buf()
    QT = sb("QT", [128, S], es=es2); b_QT = Buf()
    KT = sb("KT", [128, S], es=es2); b_KT = Buf()
    Vsb = sb("Vsb", [128, NLT * 129], es=es2); b_V = Buf()
    scal = sb("scal", [1, 64], F32, es=es2); b_scal = Buf()
    scalb = sb("scalb", [1, 4], es=es2); b_sclb = Buf()
    biasm = sb("biasm", [128, 2], F32, es=es2); b_bm = Buf()
    satr = sb("satr", [128, 4 * S], es=es2); b_sat = Buf()
    qdst = sb("qdst", [128, 2 * EXTW], es=es2)
    b_qd = [[Buf() for _ in range(6)] for _ in range(2)]
    QPE = [sb(f"QPE{h}", [128, NLTQ * S], es=es2) for h in range(2)]
    b_QPE = [[Buf() for _ in range(NLTQ)] for _ in range(2)]
    kdst = sb("kdst", [128, 4 * KWIN], es=es2)
    b_kd_sb = [[[Buf() for _ in range(3)] for _ in range(2)] for _ in range(2)]
    KPE = sb("KPE", [128, 6 * LQ], es=es2)
    b_KPE = [[Buf(), Buf(), Buf()], [Buf(), Buf(), Buf()]]
    PT = sb("PT", [128, 4 * LQ], es=es2); b_PT = [[Buf(), Buf()], [Buf(), Buf()]]
    ctxsb = sb("ctxsb", [65, 2 * LQ], F32, es=es2); b_ctxsb = [Buf(), Buf()]
    trp = sb("trp", [128, 8 * 65], F32, es=es2)
    b_trp = [Buf() for _ in range(8)]

    ps_kd = [psum(f"ps_kd{h}", [128, 512]) for h in range(2)]
    b_kd = [Buf(), Buf()]
    ps_qd = [psum(f"ps_qd{h}", [128, 512]) for h in range(2)]
    b_qdp = [Buf(), Buf()]
    _pssc = [psum(f"ps_sc{h}", [128, LQ]) for h in range(2)]
    _bsc = [Buf(), Buf()]
    ps_sc = [[_pssc[h], _pssc[h]] for h in range(2)]
    b_sc = [[_bsc[h], _bsc[h]] for h in range(2)]
    ps_ctx = [psum(f"ps_ctx{h}", [65, LQ], es=es2) for h in range(2)]
    b_ctx = [Buf(), Buf()]

    def load_cast(dst, b_dst, src, rows, cols):
        slot = 0
        for r0 in range(0, rows, 128):
            rr = min(128, rows - r0)
            for c0 in range(0, cols, LDW_):
                cw = min(LDW_, cols - c0)
                bl = b_ld[slot]
                pr.dma("qsp", out=ld[0:rr, LDW_ * slot:LDW_ * slot + cw],
                       in_=src[r0:r0 + rr, c0:c0 + cw], writes=[bl])
                dcol = (r0 // 128) * cols + c0
                with pr.g("dve", reads=[bl], writes=[b_dst]) as g:
                    g.tensor_copy(dst[0:rr, dcol:dcol + cw],
                                  ld[0:rr, LDW_ * slot:LDW_ * slot + cw])
                slot ^= 1

    pr.dma("qsp", out=ident[:], in_=ident_d, writes=[b_id])
    pr.dma("qsp", out=idf32[:], in_=identf_d, writes=[b_idf])
    pr.dma("qsp", out=ones[:], in_=ones_d, writes=[b_1])
    load_cast(xT, b_xT, x_fm, E, S)
    load_cast(wbuf[0], b_w[0], wqT_d, E, E)
    load_cast(wbuf[1], b_w[1], wkT_d, E, E)
    load_cast(wbuf[2], b_w[2], wvT_d, E, E)
    for (t_, b_, src) in [(biasq, b_bq, bq_d), (biask, b_bk, bk_d), (biasv, b_bv, bv_d)]:
        pr.dma("qsp", out=ld[0:1, 0:E], in_=src, writes=[b_ld[0]])
        with pr.g("dve", reads=[b_ld[0]], writes=[b_]) as g:
            g.tensor_copy(t_[:], ld[0:1, 0:E])
    load_cast(eext, b_ee, eext_d, 128, EXTW)
    load_cast(erev, b_er, erev_d, 128, EXTW)
    for t in range(NLT):
        with pr.g("dve", reads=[b_1], writes=[b_V]) as g:
            g.tensor_copy(Vsb[:, 129 * t + 64:129 * t + 65], ones[:, 0:1])

    def chunk_max(dst_col, row_sb, b_row, hb, width):
        nch = (width + 511) // 512
        for k, cc in enumerate(range(0, width, 512)):
            cw = min(512, width - cc)
            with pr.g("pe", reads=[b_row, b_1], writes=[b_kd[0]]) as g:
                g.matmul(ps_kd[0][0:1, 0:cw], ones[hb:hb + 64, 0:1],
                         row_sb[hb:hb + 64, cc:cc + cw], start=True, stop=True)
            with pr.g("dve", reads=[b_kd[0]], writes=[b_scal]) as g:
                g.tensor_reduce(scal[0:1, 56 + k:57 + k], ps_kd[0][0:1, 0:cw],
                                axis=AX, op=ALU.max)
        with pr.g("dve", reads=[b_scal], writes=[b_scal]) as g:
            g.tensor_reduce(scal[0:1, dst_col:dst_col + 1],
                            scal[0:1, 56:56 + nch], axis=AX, op=ALU.max)

    with pr.g("dve", writes=[x for sub in b_qd for x in sub]) as g:
        g.memset(qdst[:], 0.0)
    with pr.g("dve", writes=[x for sl in b_kd_sb for sub in sl for x in sub]) as g:
        g.memset(kdst[:], 0.0)
    for h in range(2):
        with pr.g("dve", writes=list(b_QPE[h])) as g:
            g.memset(QPE[h][:], 0.0)
    with pr.g("dve", writes=[bb for pair in b_KPE for bb in pair]) as g:
        g.memset(KPE[:], 0.0)
    with pr.g("dve", writes=[bb for pair in b_PT for bb in pair]) as g:
        g.memset(PT[:], 0.0)
    with pr.g("dve", writes=[b_sat]) as g:
        g.memset(satr[:], 0.0)
    with pr.g("dve", writes=[b_scal]) as g:
        g.memset(scal[:], 0.0)

    with pr.g("act", reads=[b_ee], writes=[b_qd[0][0]]) as g:
        g.activation(qdst[0:64, 0:JW], eext[0:64, PAD:PAD + JW], AF.Square)
    chunk_max(32, qdst, b_qd[0][0], 0, JW)

    def projections(p):
        f0 = 128 * p
        for (wi, bias_sb, bb, dstT, bdst) in [(0, biasq, b_bq, QT, b_QT),
                                              (1, biask, b_bk, KT, b_KT)]:
            for lc in range(0, S, 512):
                with pr.g("pe", reads=[b_w[wi], b_xT, bb, b_1], writes=[b_kd[wi]]) as g:
                    for ec in range(NEC):
                        g.matmul(ps_kd[wi][:, 0:512],
                                 wbuf[wi][:, ec * E + f0:ec * E + f0 + 128],
                                 xT[:, ec * S + lc:ec * S + lc + 512],
                                 start=(ec == 0), stop=False)
                    g.matmul(ps_kd[wi][:, 0:512], bias_sb[0:1, f0:f0 + 128],
                             ones[0:1, 0:512], start=False, stop=True)
                with pr.g("dve", reads=[b_kd[wi]], writes=[bdst]) as g:
                    g.tensor_copy(dstT[:, lc:lc + 512], ps_kd[wi][:, 0:512])
        for t in range(NLT):
            sl = t % 2
            with pr.g("pe", reads=[b_w[2], b_xT, b_bv, b_1], writes=[b_kd[sl]]) as g:
                for ec in range(NEC):
                    g.matmul(ps_kd[sl][:, 0:128],
                             xT[:, ec * S + 128 * t:ec * S + 128 * t + 128],
                             wbuf[2][:, ec * E + f0:ec * E + f0 + 128],
                             start=(ec == 0), stop=False)
                g.matmul(ps_kd[sl][:, 0:128], ones[0:1, 0:128],
                         biasv[0:1, f0:f0 + 128], start=False, stop=True)
            with pr.g("dve", reads=[b_kd[sl]], writes=[b_V]) as g:
                g.tensor_copy(Vsb[:, 129 * t + 0:129 * t + 64], ps_kd[sl][:, 0:64])
                g.tensor_copy(Vsb[:, 129 * t + 65:129 * t + 129], ps_kd[sl][:, 64:128])

    def bounds(p):
        for hh in range(2):
            hb = 64 * hh
            o = 16 * hh
            for (T, bT, col) in [(QT, b_QT, 0), (KT, b_KT, 1)]:
                with pr.g("act", reads=[bT], writes=[b_qd[0][0]]) as g:
                    g.activation(qdst[hb:hb + 64, 0:S], T[hb:hb + 64, 0:S], AF.Square)
                chunk_max(o + col, qdst, b_qd[0][0], hb, S)
            with pr.g("act", reads=[b_scal], writes=[b_scal]) as g:
                g.activation(scal[0:1, o + 2:o + 3], scal[0:1, o + 0:o + 1], AF.Ln)
                g.activation(scal[0:1, o + 3:o + 4], scal[0:1, o + 1:o + 2], AF.Ln)
                g.activation(scal[0:1, o + 4:o + 5], scal[0:1, 32:33], AF.Ln)
            with pr.g("dve", reads=[b_scal], writes=[b_scal]) as g:
                g.tensor_tensor(scal[0:1, o + 5:o + 6], scal[0:1, o + 2:o + 3],
                                scal[0:1, o + 3:o + 4], ALU.add)
                g.tensor_tensor(scal[0:1, o + 6:o + 7], scal[0:1, o + 2:o + 3],
                                scal[0:1, o + 4:o + 5], ALU.add)
                g.tensor_tensor(scal[0:1, o + 7:o + 8], scal[0:1, o + 3:o + 4],
                                scal[0:1, o + 4:o + 5], ALU.add)
            with pr.g("act", reads=[b_scal], writes=[b_scal]) as g:
                g.activation(scal[0:1, o + 8:o + 9], scal[0:1, o + 5:o + 6],
                             AF.Exp, scale=0.5)
                g.activation(scal[0:1, o + 9:o + 10], scal[0:1, o + 6:o + 7],
                             AF.Exp, scale=0.5)
                g.activation(scal[0:1, o + 10:o + 11], scal[0:1, o + 7:o + 8],
                             AF.Exp, scale=0.5)
            with pr.g("dve", reads=[b_scal], writes=[b_scal]) as g:
                g.tensor_tensor(scal[0:1, o + 11:o + 12], scal[0:1, o + 8:o + 9],
                                scal[0:1, o + 9:o + 10], ALU.add)
            with pr.g("dve", reads=[b_scal], writes=[b_scal]) as g:
                g.tensor_tensor(scal[0:1, o + 12:o + 13], scal[0:1, o + 11:o + 12],
                                scal[0:1, o + 10:o + 11], ALU.add)
            with pr.g("dve", reads=[b_scal], writes=[b_scal]) as g:
                g.tensor_scalar(scal[0:1, o + 13:o + 14], scal[0:1, o + 12:o + 13],
                                -0.125, None, ALU.mult)
            with pr.g("dve", reads=[b_scal], writes=[b_sclb]) as g:
                g.tensor_copy(scalb[0:1, hh:hh + 1], scal[0:1, o + 13:o + 14])
            with pr.g("pe", reads=[b_sclb, b_1], writes=[b_kd[0]]) as g:
                g.matmul(ps_kd[0][:, 0:1], ones[0:1, 0:128], scalb[0:1, hh:hh + 1],
                         start=True, stop=True)
            with pr.g("dve", reads=[b_kd[0]], writes=[b_bm]) as g:
                g.tensor_copy(biasm[:, hh:hh + 1], ps_kd[0][:, 0:1])

    def sat_rows(p):
        for hh in range(2):
            hb = 64 * hh
            jobs = [(bloc, ecol, T, bT)
                    for (bloc, ecol, T, bT) in [
                        (0, PAD, KT, b_KT), (1, PAD + JW - 1, KT, b_KT),
                        (2, PAD, QT, b_QT), (3, PAD + JW - 1, QT, b_QT)]
                    for cc in [None]]
            k = 0
            for (bloc, ecol, T, bT) in [(0, PAD, KT, b_KT), (1, PAD + JW - 1, KT, b_KT),
                                        (2, PAD, QT, b_QT), (3, PAD + JW - 1, QT, b_QT)]:
                for cc in range(0, S, 512):
                    pst, bt = (ps_kd[k % 2], b_kd[k % 2])
                    with pr.g("pe", reads=[b_ee, bT], writes=[bt]) as g:
                        g.matmul(pst[0:1, 0:512], eext[hb:hb + 64, ecol:ecol + 1],
                                 T[hb:hb + 64, cc:cc + 512], start=True, stop=True)
                    with pr.g("dve", reads=[bt], writes=[b_sat]) as g:
                        g.tensor_copy(
                            satr[hb:hb + 1, bloc * S + cc:bloc * S + cc + 512],
                            pst[0:1, 0:512])
                    k += 1

    def qd_prologue(p, lq):
        l_base = lq * LQ
        for i in range(NLTQ):
            l0 = l_base + 128 * i
            rts = [r0 for r0 in range(0, S, 128) if c.kind(l0, r0) == "band"]
            r_lo, r_hi = rts[0], rts[-1] + 128
            w_lo = r_lo - (l0 + 127) + MAXP - 1 + PAD
            w_hi = (r_hi - 1) - l0 + MAXP - 1 + PAD + 1
            assert 0 <= w_lo and w_hi <= EXTW, (w_lo, w_hi)
            c0 = w_lo
            for ci, cc in enumerate(range(c0, w_hi, 512)):
                cw = min(512, EXTW - cc)
                pst = ps_kd if ci % 2 == 0 else ps_qd
                bst = b_kd if ci % 2 == 0 else b_qdp
                with pr.g("pe", reads=[b_QT, b_er], writes=[bst[0], bst[1]]) as g:
                    for hh in range(2):
                        hb = 64 * hh
                        g.matmul(pst[hh][:, 0:cw], QT[hb:hb + 64, l0:l0 + 128],
                                 erev[hb:hb + 64, cc:cc + cw], start=True, stop=True)
                for hh in range(2):
                    ev = "dve" if hh == 0 else "act"
                    dst = qdst[:, EXTW * hh + cc:EXTW * hh + cc + cw]
                    with pr.g(ev, reads=[bst[hh]], writes=[b_qd[hh][ci % 6]]) as g:
                        if ev == "dve":
                            g.tensor_copy(dst, pst[hh][:, 0:cw])
                        else:
                            g.copy(dst, pst[hh][:, 0:cw])
            for hh in range(2):
                off = EXTW * hh + (r_lo - l0 + MAXP - 1 + PAD)
                src = AP(qdst.ap().tensor, off, [(2 * EXTW - 1, 128), (1, r_hi - r_lo)])
                pr.dma("qsp", out=QPE[hh][:, S * i + r_lo:S * i + r_hi], in_=src,
                       reads=list(b_qd[hh]), writes=[b_QPE[hh][i]])

    def kd_strip(p, lq, r0, slot):
        l_base = lq * LQ
        lts = [l0 for l0 in range(l_base, l_base + LQ, 128) if c.kind(l0, r0) == "band"]
        if not lts:
            return None
        l_lo, l_hi = lts[0], lts[-1] + 128
        u_lo = l_lo - (r0 + 127) + MAXP - 1 + PAD
        u_hi = (l_hi - 1) - r0 + MAXP - 1 + PAD + 1
        assert 0 <= u_lo and u_hi <= EXTW, (u_lo, u_hi)
        c0 = u_lo
        assert u_hi - c0 <= KWIN
        ksl = slot % 2
        for ci, cc in enumerate(range(c0, u_hi, 512)):
            cw = min(512, EXTW - cc)
            pst = ps_kd if ci % 2 == 0 else ps_qd
            bst = b_kd if ci % 2 == 0 else b_qdp
            with pr.g("pe", reads=[b_KT, b_ee], writes=[bst[0], bst[1]]) as g:
                for hh in range(2):
                    hb = 64 * hh
                    g.matmul(pst[hh][:, 0:cw], KT[hb:hb + 64, r0:r0 + 128],
                             eext[hb:hb + 64, cc:cc + cw], start=True, stop=True)
            for hh in range(2):
                ev = "dve" if hh == 0 else "act"
                dst = kdst[:, KWIN * (2 * ksl + hh) + (cc - c0):
                           KWIN * (2 * ksl + hh) + (cc - c0) + cw]
                with pr.g(ev, reads=[bst[hh]], writes=[b_kd_sb[ksl][hh][ci]]) as g:
                    if ev == "dve":
                        g.tensor_copy(dst, pst[hh][:, 0:cw])
                    else:
                        g.copy(dst, pst[hh][:, 0:cw])
        for hh in range(2):
            ksl = slot % 2
            off = KWIN * (2 * ksl + hh) + (l_lo - r0 + MAXP - 1 + PAD - c0)
            src = AP(kdst.ap().tensor, off, [(4 * KWIN - 1, 128), (1, l_hi - l_lo)])
            pr.dma("qsp",
                   out=KPE[:, (3 * hh + slot) * LQ:(3 * hh + slot) * LQ + (l_hi - l_lo)],
                   in_=src, reads=list(b_kd_sb[ksl][hh]), writes=[b_KPE[hh][slot]])
        return (l_lo, l_hi)

    def score_pair(p, lq, r0, kwin, kslot, slot):
        l_base = lq * LQ
        opsh = [[], []]
        reads = [b_KT, b_QT, b_id]
        for hh in range(2):
            hb = 64 * hh

            def qk(g, st, sp, hh=hh, hb=hb):
                g.matmul(ps_sc[hh][slot][:, 0:LQ], KT[hb:hb + 64, r0:r0 + 128],
                         QT[hb:hb + 64, l_base:l_base + LQ], start=st, stop=sp)
            opsh[hh].append(qk)
        for (side, kbloc, qbloc) in [("hi", 0, 2), ("lo", 1, 3)]:
            sls = [l0 for l0 in range(l_base, l_base + LQ, 128) if c.kind(l0, r0) == side]
            if not sls:
                continue
            s_lo, s_hi = sls[0] - l_base, sls[-1] + 128 - l_base
            reads += [b_sat, b_1]
            for hh in range(2):
                hb = 64 * hh

                def sat1(g, st, sp, s_lo=s_lo, s_hi=s_hi, kbloc=kbloc, hh=hh, hb=hb):
                    g.matmul(ps_sc[hh][slot][:, s_lo:s_hi],
                             satr[hb:hb + 1, kbloc * S + r0:kbloc * S + r0 + 128],
                             ones[hb:hb + 1, 0:s_hi - s_lo], start=st, stop=sp)

                def sat2(g, st, sp, s_lo=s_lo, s_hi=s_hi, qbloc=qbloc, hh=hh, hb=hb):
                    g.matmul(ps_sc[hh][slot][:, s_lo:s_hi], ones[hb:hb + 1, 0:128],
                             satr[hb:hb + 1, qbloc * S + l_base + s_lo:
                                  qbloc * S + l_base + s_hi], start=st, stop=sp)
                opsh[hh] += [sat1, sat2]
        if kwin is not None:
            l_lo, l_hi = kwin
            for l0 in range(l_lo, l_hi, 128):
                i = (l0 - l_base) // 128
                for hh in range(2):
                    def qop(g, st, sp, l0=l0, i=i, hh=hh):
                        g.matmul(ps_sc[hh][slot][:, l0 - l_base:l0 - l_base + 128],
                                 QPE[hh][:, S * i + r0:S * i + r0 + 128], ident[:],
                                 start=st, stop=sp)
                    opsh[hh].append(qop)
                reads += [b_QPE[0][i], b_QPE[1][i]]
        dedup, seen = [], set()
        for b in reads:
            if id(b) not in seen:
                seen.add(id(b))
                dedup.append(b)
        nops = len(opsh[0])
        with pr.g("pe", reads=dedup, writes=[b_sc[0][slot], b_sc[1][slot]]) as g:
            for k in range(nops):
                for hh in range(2):
                    opsh[hh][k](g, k == 0, k == nops - 1 and kwin is None)
        if kwin is not None:
            with pr.g("pe", reads=[b_KPE[0][kslot], b_KPE[1][kslot], b_id],
                      writes=[b_sc[0][slot], b_sc[1][slot]]) as g:
                for hh in range(2):
                    g.matmul(ps_sc[hh][slot][:, l_lo - l_base:l_hi - l_base], ident[:],
                             KPE[:, (3 * hh + kslot) * LQ:(3 * hh + kslot) * LQ + (l_hi - l_lo)],
                             start=False, stop=True)

    def exp_tile(p, lq, r0, hh, slot):
        with pr.g("act", reads=[b_sc[hh][slot], b_bm], writes=[b_PT[hh][slot]]) as g:
            g.activation(PT[:, (2 * hh + slot) * LQ:(2 * hh + slot) * LQ + LQ],
                         ps_sc[hh][slot][:, 0:LQ], AF.Exp,
                         bias=biasm[:, hh:hh + 1], scale=0.125)

    def pv_pair(p, lq, r0, slot, first, last):
        t = r0 // 128
        with pr.g("pe", reads=[b_V, b_PT[0][slot], b_PT[1][slot]],
                  writes=[b_ctx[0], b_ctx[1]]) as g:
            for hh in range(2):
                vcol = 129 * t + (0 if hh == 0 else 64)
                g.matmul(ps_ctx[hh][:, 0:LQ], Vsb[:, vcol:vcol + 65],
                         PT[:, (2 * hh + slot) * LQ:(2 * hh + slot) * LQ + LQ],
                         start=first, stop=last)

    def ctx_out(p, lq):
        l_base = lq * LQ
        for hh in range(2):
            h = 2 * p + hh
            with pr.g("dve", reads=[b_ctx[hh]], writes=[b_ctxsb[hh]]) as g:
                g.tensor_copy(ctxsb[:, hh * LQ:hh * LQ + LQ], ps_ctx[hh][:, 0:LQ])
            for i in range(NLTQ):
                l0 = l_base + 128 * i
                tsl = 4 * hh + i % 4
                with pr.g("pe", reads=[b_ctxsb[hh], b_idf], writes=[b_kd[hh]]) as g:
                    g.transpose(ps_kd[hh][:, 0:65],
                                ctxsb[0:65, hh * LQ + 128 * i:hh * LQ + 128 * i + 128],
                                idf32[0:65, 0:65])
                with pr.g("dve", reads=[b_kd[hh]], writes=[b_trp[tsl]]) as g:
                    g.tensor_copy(trp[:, 65 * tsl:65 * tsl + 65], ps_kd[hh][:, 0:65])
                pr.dma("qsp", out=ctx_dram[l0:l0 + 128, 65 * h:65 * h + 65],
                       in_=trp[:, 65 * tsl:65 * tsl + 65],
                       reads=[b_trp[tsl], b_ctxd[l0 // 128]])

    for p in range(c.NPAIR if stages >= 90 else (1 if stages >= 2 else 0)):
        projections(p)
        if stages >= 3:
            bounds(p)
        if stages >= 4:
            sat_rows(p)
        if p == 0:
            dump("QT0", QT[:], (128, S), BF16, [b_QT])
            dump("KT0", KT[:], (128, S), BF16, [b_KT])
            dump("Vsb", Vsb[:], (128, NLT * 129), BF16, [b_V])
            dump("satr", satr[:], (128, 4 * S), BF16, [b_sat])
            dump("biasm", biasm[:], (128, 2), F32, [b_bm])
        nlq_eff = NLQ if stages >= 80 else (1 if stages >= 5 else 0)
        for lq in range(nlq_eff):
            qd_prologue(p, lq)
            if p == 0 and lq == 0:
                dump("QPE0", QPE[0][:], (128, NLTQ * S), BF16, b_QPE[0])
            rmax = S if stages >= 80 else (128 if stages >= 6 else 0)
            l_mid = lq * LQ + LQ // 2
            r0s = sorted(range(0, rmax, 128), key=lambda r: -abs(r + 64 - l_mid))
            kwins = {}
            if rmax:
                kwins[r0s[0]] = kd_strip(p, lq, r0s[0], 0)
            if len(r0s) > 1:
                kwins[r0s[1]] = kd_strip(p, lq, r0s[1], 1)
            for ri, r0 in enumerate(r0s):
                slot = ri % 2
                if p == 0 and lq == 0 and r0 == 0:
                    dump("KPE0", KPE[:], (128, 6 * LQ), BF16,
                         [bb for pair in b_KPE for bb in pair])
                if stages >= 7:
                    score_pair(p, lq, r0, kwins[r0], ri % 3, slot)
                    if stages >= 71:
                        for hh in range(2):
                            exp_tile(p, lq, r0, hh, slot)
                if ri + 2 < len(r0s):
                    kwins[r0s[ri + 2]] = kd_strip(p, lq, r0s[ri + 2], (ri + 2) % 3)
                if stages >= 72:
                    pv_pair(p, lq, r0, slot, ri == 0, ri == len(r0s) - 1)
                if p == 0 and lq == 0 and r0 == 0:
                    dump("PT0", PT[:], (128, 4 * LQ), BF16,
                         [bb for pair in b_PT for bb in pair])
            if stages >= 80:
                ctx_out(p, lq)

    es2.close()

    resid = sb("resid", [128, 2 * E], F32, es=es3)
    b_res2 = [Buf(), Buf()]
    sqf = sb("sqf", [128, 2 * E], F32, es=es3)
    b_sqf2 = [Buf(), Buf()]
    ctxt = sb("ctxt", [128, 2 * c.H * 65], F32, es=es3)
    b_ct2 = [Buf(), Buf()]
    xt_sb = sb("xt_sb", [128, 4 * E], F32, es=es3)
    b_xts2 = [Buf() for _ in range(4)]
    ln1bf = sb("ln1bf", [128, 2 * E], es=es3)
    b_l1b2 = [Buf(), Buf()]
    ln1T = sb("ln1T", [128, NEC * S], es=es3)
    b_l1T = [Buf() for _ in range(NLT)]
    ps_tb = psum("ps_tb", [128, 2048], BF16, es=es3)
    b_tb = [Buf(), Buf()]

    def ln_stats(b_src, src_sb, sl=0):
        b_st = b_st2[sl]
        so = 16 * sl
        nh = (E + 511) // 512
        with pr.g("dve", reads=[b_src], writes=[b_st]) as g:
            for hhh in range(nh):
                w = min(512, E - 512 * hhh)
                g.bn_stats(stat[:, so + 6 * hhh:so + 6 * hhh + 6],
                           src_sb[:, 512 * hhh:512 * hhh + w])
        with pr.g("dve", reads=[b_st], writes=[b_st]) as g:
            g.bn_aggr(stat[:, so + 12:so + 14], stat[:, so:so + 6 * nh])
        with pr.g("dve", reads=[b_st], writes=[b_st]) as g:
            g.tensor_scalar(stat[:, so + 14:so + 15], stat[:, so + 13:so + 14], 1.0,
                            1e-5, ALU.mult, ALU.add)
        with pr.g("act", reads=[b_st], writes=[b_st]) as g:
            g.activation(stat[:, so + 15:so + 16], stat[:, so + 14:so + 15], AF.Ln)
        with pr.g("act", reads=[b_st], writes=[b_st]) as g:
            g.activation(stat[:, so + 10:so + 11], stat[:, so + 15:so + 16],
                         AF.Exp, scale=-0.5)
        with pr.g("dve", reads=[b_src, b_st], writes=[b_src]) as g:
            g.tensor_scalar(src_sb, src_sb, stat[:, so + 12:so + 13],
                            stat[:, so + 10:so + 11], ALU.subtract, ALU.mult)

    HC = c.H * 65

    def ln1_tile(t):
        l0 = 128 * t
        sl = t % 2
        b_xts, b_ct = b_xts2[sl], b_ct2[sl]
        b_res = b_res2[sl]
        rs_v = resid[:, sl * E:sl * E + E]
        pr.dma("qsp", out=xt_sb[:, sl * E:sl * E + E], in_=x_tok[l0:l0 + 128, 0:E],
               writes=[b_xts])
        pr.dma("qsp", out=ctxt[:, sl * HC:sl * HC + HC],
               in_=ctx_dram[l0:l0 + 128, 0:HC], writes=[b_ct, b_ctxd[t]])
        with pr.g("dve", reads=[b_ct], writes=[b_ct]) as g:
            for h in range(c.H):
                zc = sl * HC + 65 * h + (64 if h % 2 == 0 else 0)
                g.reciprocal(ctxt[:, zc:zc + 1], ctxt[:, zc:zc + 1])
        with pr.g("dve", reads=[b_ct], writes=[b_res]) as g:
            for h in range(c.H):
                zc = sl * HC + 65 * h + (64 if h % 2 == 0 else 0)
                cc = sl * HC + 65 * h + (0 if h % 2 == 0 else 1)
                g.tensor_scalar(rs_v[:, 64 * h:64 * h + 64], ctxt[:, cc:cc + 64],
                                ctxt[:, zc:zc + 1], None, ALU.mult)
        with pr.g("dve", reads=[b_xts, b_res], writes=[b_res]) as g:
            g.tensor_tensor(rs_v, rs_v, xt_sb[:, sl * E:sl * E + E], ALU.add)
        ln_stats(b_res, rs_v, sl)
        pr.dma("qsp", out=ln1_dram[l0:l0 + 128, 0:E], in_=rs_v, reads=[b_res],
               writes=[b_ln1d[t]])
        with pr.g("dve", reads=[b_res], writes=[b_l1b2[sl]]) as g:
            g.tensor_copy(ln1bf[:, sl * E:sl * E + E], rs_v)
        for ec in range(NEC):
            esl = ec % 2
            with pr.g("pe", reads=[b_l1b2[sl], b_id], writes=[b_tb[esl]]) as g:
                g.transpose(ps_tb[:, 1024 * esl:1024 * esl + 128],
                            ln1bf[:, sl * E + 128 * ec:sl * E + 128 * ec + 128], ident[:])
            with pr.g("dve", reads=[b_tb[esl]], writes=[b_l1T[t]]) as g:
                g.tensor_copy(ln1T[:, ec * S + l0:ec * S + l0 + 128],
                              ps_tb[:, 1024 * esl:1024 * esl + 128])
        if t == 0:
            dump("ln1_0", resid[:, 0:E], (128, E), F32, [b_res2[0]])

    load_cast(wbuf[0], b_w[0], wr_d, E, E)
    load_cast(wbuf[1], b_w[1], wkf_d, E, E)
    load_cast(wbuf[2], b_w[2], wvf_d, E, E)
    mur = sb("mur_sb", [128, NEC], F32, es=es3); b_mur = Buf()
    muk = sb("muk_sb", [128, NEC], F32, es=es3); b_muk = Buf()
    murc = sb("murc", [128, 2 * NEC], F32, es=es3); b_murc = Buf()
    for (t_, b_, src) in [(mur, b_mur, mur_d), (muk, b_muk, muk_d)]:
        pr.dma("qsp", out=ld[0:128, 0:NEC], in_=src, writes=[b_ld[0]])
        with pr.g("dve", reads=[b_ld[0]], writes=[b_]) as g:
            g.tensor_copy(t_[:], ld[0:128, 0:NEC])
    with pr.g("dve", reads=[b_mur, b_muk], writes=[b_murc]) as g:
        g.tensor_scalar(murc[:, 0:NEC], mur[:], -1.0, 1.0, ALU.mult, ALU.add)
        g.tensor_scalar(murc[:, NEC:2 * NEC], muk[:], -1.0, 1.0, ALU.mult, ALU.add)

    LB = LQ
    xrT = sb("xrT", [128, NEC * LB], es=es3); b_xr = Buf()
    xkT = sb("xkT", [128, NEC * LB], es=es3); b_xk = Buf()
    rsb = sb("rsb", [128, NEC * LB], F32, es=es3); b_r = Buf()
    kact = sb("kact", [128, NEC * LB], es=es3); b_ka = Buf()
    ffnT = sb("ffnT", [128, NEC * LB], F32, es=es3); b_ff = Buf()
    otile = sb("otile", [128, (512 // 128) * E], F32, es=es3); b_ot = Buf()

    def ffn_block(bl):
        l0 = bl * LB
        t_lo = max(0, bl * LB // 128 - 1)
        t_hi = min(NLT, (bl * LB + LB) // 128)
        l1deps = [b_l1T[tt] for tt in range(t_lo, t_hi)]
        with pr.g("dve", reads=l1deps + [b_murc], writes=[b_xr, b_xk]) as g:
            for ec in range(NEC):
                if l0 == 0:
                    g.memset(xrT[:, ec * LB:ec * LB + 1], 0.0)
                    g.memset(xkT[:, ec * LB:ec * LB + 1], 0.0)
                    d0, sh = 1, 0
                else:
                    d0, sh = 0, -1
                g.tensor_scalar(xrT[:, ec * LB + d0:ec * LB + LB],
                                ln1T[:, ec * S + l0 + sh:ec * S + l0 + LB - 1],
                                murc[:, ec:ec + 1], None, ALU.mult)
                g.tensor_scalar(xkT[:, ec * LB + d0:ec * LB + LB],
                                ln1T[:, ec * S + l0 + sh:ec * S + l0 + LB - 1],
                                murc[:, NEC + ec:NEC + ec + 1], None, ALU.mult)
        with pr.g("dve", reads=l1deps + [b_mur, b_muk, b_xr, b_xk],
                  writes=[b_xr, b_xk]) as g:
            for ec in range(NEC):
                g.scalar_tensor_tensor(xrT[:, ec * LB:ec * LB + LB],
                                       ln1T[:, ec * S + l0:ec * S + l0 + LB],
                                       mur[:, ec:ec + 1], xrT[:, ec * LB:ec * LB + LB],
                                       ALU.mult, ALU.add)
                g.scalar_tensor_tensor(xkT[:, ec * LB:ec * LB + LB],
                                       ln1T[:, ec * S + l0:ec * S + l0 + LB],
                                       muk[:, ec:ec + 1], xkT[:, ec * LB:ec * LB + LB],
                                       ALU.mult, ALU.add)
        for dc in range(NEC):
            pr1, br1 = (ps_kd[0], b_kd[0]) if dc % 2 == 0 else (ps_qd[0], b_qdp[0])
            pr2, br2 = (ps_kd[1], b_kd[1]) if dc % 2 == 0 else (ps_qd[1], b_qdp[1])
            with pr.g("pe", reads=[b_w[0], b_xr], writes=[br1]) as g:
                for ec in range(NEC):
                    g.matmul(pr1[:, 0:LB],
                             wbuf[0][:, ec * E + 128 * dc:ec * E + 128 * dc + 128],
                             xrT[:, ec * LB:ec * LB + LB],
                             start=(ec == 0), stop=(ec == NEC - 1))
            with pr.g("act", reads=[br1], writes=[b_r]) as g:
                g.activation(rsb[:, dc * LB:dc * LB + LB], pr1[:, 0:LB], AF.Sigmoid)
            with pr.g("pe", reads=[b_w[1], b_xk], writes=[br2]) as g:
                for ec in range(NEC):
                    g.matmul(pr2[:, 0:LB],
                             wbuf[1][:, ec * E + 128 * dc:ec * E + 128 * dc + 128],
                             xkT[:, ec * LB:ec * LB + LB],
                             start=(ec == 0), stop=(ec == NEC - 1))
            with pr.g("act", reads=[br2], writes=[b_ka]) as g:
                g.activation(kact[:, dc * LB:dc * LB + LB], pr2[:, 0:LB], AF.Relu)
            with pr.g("act", reads=[b_ka], writes=[b_ka]) as g:
                g.activation(kact[:, dc * LB:dc * LB + LB],
                             kact[:, dc * LB:dc * LB + LB], AF.Square)
        NSUB = LB // 128
        for i in range(NSUB):
            lt = l0 + 128 * i
            pr.dma("qsp", out=xt_sb[:, i * E:i * E + E],
                   in_=ln1_dram[lt:lt + 128, 0:E], reads=[b_ln1d[lt // 128]],
                   writes=[b_xts2[i]])
        b_ffc = [Buf() for _ in range(NEC)]
        for dc in range(NEC):
            psv, bsv = (ps_sc[0][1], b_sc[0][1]) if dc % 2 == 0 else (ps_sc[1][1], b_sc[1][1])
            with pr.g("pe", reads=[b_w[2], b_ka], writes=[bsv]) as g:
                for ec in range(NEC):
                    g.matmul(psv[:, 0:LB],
                             wbuf[2][:, ec * E + 128 * dc:ec * E + 128 * dc + 128],
                             kact[:, ec * LB:ec * LB + LB],
                             start=(ec == 0), stop=(ec == NEC - 1))
            with pr.g("dve", reads=[bsv, b_r], writes=[b_ffc[dc]]) as g:
                g.tensor_tensor(ffnT[:, dc * LB:dc * LB + LB], psv[:, 0:LB],
                                rsb[:, dc * LB:dc * LB + LB], ALU.mult)
            for i in range(NSUB):
                sl = (dc + i) % 2
                with pr.g("pe", reads=[b_ffc[dc], b_idf], writes=[b_kd[sl]]) as g:
                    g.transpose(ps_kd[sl][:, 0:128],
                                ffnT[0:128, dc * LB + 128 * i:dc * LB + 128 * i + 128],
                                idf32[:])
                with pr.g("dve", reads=[b_kd[sl], b_xts2[i]], writes=[b_ot]) as g:
                    g.tensor_tensor(otile[:, i * E + 128 * dc:i * E + 128 * dc + 128],
                                    ps_kd[sl][:, 0:128],
                                    xt_sb[:, i * E + 128 * dc:i * E + 128 * dc + 128],
                                    ALU.add)
        for i in range(NSUB):
            lt = l0 + 128 * i
            ln_stats(b_ot, otile[:, i * E:i * E + E], i % 2)
            pr.dma("qsp", out=out_t[lt:lt + 128, 0:E], in_=otile[:, i * E:i * E + E],
                   reads=[b_ot])
        if bl == 0:
            dump("xrT0", xrT[:], (128, NEC * LB), BF16, [b_xr])
            dump("ffnT0", ffnT[:], (128, NEC * LB), F32, b_ffc)

    TPB = LB // 128
    if stages >= 100:
        for t in range(NLT):
            ln1_tile(t)
            if stages >= 110 and (t + 1) % TPB == 0:
                ffn_block((t + 1) // TPB - 1)

    for i in range(4):
        if pr.cnt[f"qsp{i}"]:
            nc.sync.wait_ge(pr.sem[f"qsp{i}"], pr.cnt[f"qsp{i}"])
        if pr.cnt[f"qact{i}"]:
            nc.sync.wait_ge(pr.sem[f"qact{i}"], pr.cnt[f"qact{i}"])
    es3.close()
    pr.es.close()
    return nc, dbg_used


def host_inputs(cfg, x_b, wq, bq, wk, bk, wv, bv, dist, W_r, W_k, W_v, mu_r, mu_k):
    c = cfg
    ar = lambda a: np.ascontiguousarray(a, np.float32)
    idx = np.arange(c.EXTW) - c.PAD
    e_ext = dist[np.clip(idx, 0, c.JW - 1)]
    e_rev = dist[np.clip(c.JW - 1 - idx, 0, c.JW - 1)]
    em = {
        "x_tok": ar(x_b), "x_fm": ar(x_b.T),
        "wqT": ar(wq.T), "wkT": ar(wk.T), "wvT": ar(wv.T),
        "bqv": ar(bq.reshape(1, -1)), "bkv": ar(bk.reshape(1, -1)),
        "bvv": ar(bv.reshape(1, -1)),
        "e_ext": ar(np.concatenate([e_ext.T, e_ext.T], axis=0)),
        "e_rev": ar(np.concatenate([e_rev.T, e_rev.T], axis=0)),
        "wrT": ar(W_r.T), "wkfT": ar(W_k.T), "wvfT": ar(W_v.T),
        "mur": ar(mu_r.reshape(c.NEC, 128).T),
        "muk": ar(mu_k.reshape(c.NEC, 128).T),
        "ident": np.eye(128, dtype=BF),
        "identf": np.eye(128, dtype=np.float32),
        "ones_bf": np.ones((128, 512), dtype=BF),
    }
    return em


def kernel(L0_attn_hidden_states, L0_attn_wq, L0_attn_bq, L0_attn_wk, L0_attn_bk,
           L0_attn_wv, L0_attn_bv, L0_attn_dist_emb, L0_ffn_W_r, L0_ffn_W_k,
           L0_ffn_W_v, L0_ffn_mu_r, L0_ffn_mu_k):
    cfg = Cfg(S=2048, E=1024, MAXP=1024)
    f = lambda a: np.asarray(a, np.float32)
    x = np.ascontiguousarray(f(L0_attn_hidden_states))
    nc, _ = build_program(cfg)
    in_maps = [host_inputs(cfg, x[b], f(L0_attn_wq), f(L0_attn_bq), f(L0_attn_wk),
                           f(L0_attn_bk), f(L0_attn_wv), f(L0_attn_bv),
                           f(L0_attn_dist_emb), f(L0_ffn_W_r), f(L0_ffn_W_k),
                           f(L0_ffn_W_v), f(L0_ffn_mu_r), f(L0_ffn_mu_k))
               for b in range(8)]
    res = run_bass_kernel_spmd(nc, in_maps, core_ids=list(range(8)))
    return np.stack([np.asarray(res.results[b]["out"], np.float32) for b in range(8)])
